# version 25
# baseline (speedup 1.0000x reference)
import contextlib

import numpy as np
import ml_dtypes
import concourse.bass as bass
import concourse.mybir as mybir
from concourse.bass_utils import run_bass_kernel_spmd

F32 = mybir.dt.float32
BF16 = mybir.dt.bfloat16
I32 = mybir.dt.int32
AF = mybir.ActivationFunctionType
ALU = mybir.AluOpType
AX = mybir.AxisListType
ds = bass.ds

D = 1024
B = 8
L = 4096
NCORE = 8
TOK = 512
NK = 8
NTC = 4
HB = 4
OM = 64
NCODE = 256
NPAR = OM * (OM - 1) // 2
LN_EPS = 1e-5
RMS_EPS = 1e-6
NEWT = 4

DEBUG = False
PHASE = 4
SUB2 = 2
SUB4 = 5
SUB4R = 2
SUB4S = 3
PART_BYTES = 207 * 1024


def _triu_off(r):
    return r * 63 - (r * (r - 1)) // 2


class Sched:
    CURNC = None

    """Two-pass emitter: pass 1 counts semaphore values and records labels,
    pass 2 emits instructions resolving waits from the recorded labels."""

    def __init__(self, emit, labels):
        self.emit = emit
        self.cnt = {}
        self.labels = labels
        self.sems = {}
        self.cur = None
        self.curname = None

    def stream(self, name, eng):
        self.curname = name
        self.cur = eng

    def inc(self, sem, n=1):
        self.cnt[sem] = self.cnt.get(sem, 0) + n
        return self.cnt[sem]

    def mark(self, label, sem=None):
        sem = sem or self.curname
        self.labels[label] = (sem, self.cnt.get(sem, 0))

    def w(self, label):
        if self.emit:
            sem, val = self.labels[label]
            if val > 0:
                self.cur.wait_ge(self.sems[sem], val)

    def wv(self, sem, val):
        if self.emit and val > 0:
            self.cur.wait_ge(self.sems[sem], val)

    SERIAL = ("aE", "vE", "pE")

    TAGLOG = {}

    def _tag(self, info):
        if self.emit and Sched.CURNC is not None:
            Sched.TAGLOG[Sched.CURNC.n_instructions()] = info

    def op(self, fn, inc=True):
        if self.curname in self.SERIAL:
            self.wv(self.curname, self.cnt.get(self.curname, 0))
            v = self.inc(self.curname, 1)
            if self.emit:
                fn(self.cur).then_inc(self.sems[self.curname], 1)
                self._tag((self.curname, v))
            return v
        v = self.inc(self.curname, 1) if inc else None
        if self.emit:
            ins = fn(self.cur)
            if inc:
                ins.then_inc(self.sems[self.curname], 1)
            self._tag((self.curname, v))
        return v

    def dma(self, qsem, fn):
        if self.curname in self.SERIAL:
            self.wv(self.curname, self.cnt.get(self.curname, 0))
        v = self.inc(qsem, 16)
        if self.emit:
            fn(self.cur).then_inc(self.sems[qsem], 16)
            self._tag((qsem, v))
        return v


def build_program(detect_races=True):
    labels = {}
    s1 = Sched(False, labels)
    _emit(s1, None)
    nc = bass.Bass("TRN2", target_bir_lowering=False,
                   detect_race_conditions=detect_races)
    Sched.CURNC = nc
    s2 = Sched(True, dict(labels))
    _emit(s2, nc)
    Sched.CURNC = None
    for k in set(s1.cnt) | set(s2.cnt):
        assert s1.cnt.get(k, 0) == s2.cnt.get(k, 0), (
            "pass count mismatch", k, s1.cnt.get(k, 0), s2.cnt.get(k, 0))
    lab2 = {}
    s3 = Sched(False, lab2)
    _emit(s3, None)
    for k, v in lab2.items():
        assert labels[k] == v, ("label drift", k, labels[k], v)
    return nc


def _emit(sch, nc):
    emit = sch.emit
    ctx = contextlib.ExitStack()

    io = {}
    if emit:
        dt = nc.dram_tensor
        io["XS"] = dt("XS", [B, TOK, D], F32, kind="ExternalInput")
        io["XH"] = dt("XH", [B, L, D], F32, kind="ExternalInput")
        io["W3T"] = dt("W3T", [D, 3 * D], F32, kind="ExternalInput")
        io["U3T"] = dt("U3T", [D, 3 * D], F32, kind="ExternalInput")
        io["WOTR"] = dt("WOTR", [128, NK * D], F32, kind="ExternalInput")
        io["UOTR"] = dt("UOTR", [128, NK * D], F32, kind="ExternalInput")
        io["B3"] = dt("B3", [1, 3 * D], F32, kind="ExternalInput")
        io["BO"] = dt("BO", [1, D], F32, kind="ExternalInput")
        io["INC"] = dt("INC", [1, D], F32, kind="ExternalInput")
        io["OPWT"] = dt("OPWT", [D, NPAR], F32, kind="ExternalInput")
        io["OPB"] = dt("OPB", [1, NPAR], F32, kind="ExternalInput")
        io["OOWT"] = dt("OOWT", [OM, D], F32, kind="ExternalInput")
        io["OOB"] = dt("OOB", [1, D], F32, kind="ExternalInput")
        io["VQWT"] = dt("VQWT", [D, OM], F32, kind="ExternalInput")
        io["VQB"] = dt("VQB", [1, OM], F32, kind="ExternalInput")
        io["PWT"] = dt("PWT", [OM, D], F32, kind="ExternalInput")
        io["PB"] = dt("PB", [1, D], F32, kind="ExternalInput")
        io["CB"] = dt("CB", [NCODE, OM], F32, kind="ExternalInput")
        io["NW"] = dt("NW", [1, D], F32, kind="ExternalInput")
        io["MIX"] = dt("MIX", [1, 2], F32, kind="ExternalInput")
        io["IDT"] = dt("IDT", [128, 128], F32, kind="ExternalInput")
        io["OUT"] = dt("OUT", [B, TOK, D], F32, kind="ExternalOutput")
        io["xn_sc"] = dt("xn_sc", [B, TOK, D], BF16, kind="Internal")
        io["hm_sc"] = dt("hm_sc", [B, D], F32, kind="Internal")
        io["vq_sc"] = dt("vq_sc", [B, D], F32, kind="Internal")
        if DEBUG:
            io["DBG_XNB"] = dt("DBG_XNB", [128, NK * TOK], BF16,
                               kind="ExternalOutput")
            io["DBG_CF"] = dt("DBG_CF", [128, NK * TOK], F32,
                              kind="ExternalOutput")
            io["DBG_HM"] = dt("DBG_HM", [128, 64], F32, kind="ExternalOutput")
            io["DBG_G"] = dt("DBG_G", [OM, B * OM], F32, kind="ExternalOutput")
            io["DBG_VQ"] = dt("DBG_VQ", [B, D], F32, kind="ExternalOutput")
            io["DBG_S"] = dt("DBG_S", [128, NTC * D], F32,
                             kind="ExternalOutput")

    offs = {}
    base_addr = [0]
    if emit:
        guard = ctx.enter_context(
            nc.sbuf_tensor("arena", [128, PART_BYTES], mybir.dt.uint8))
        base_addr[0] = nc.lookup_mloc(guard).addr

    def carve(name, off, shape, dtype):
        per = int(np.prod(shape[1:])) * mybir.dt.size(dtype)
        assert off % 32 == 0 and off + per <= PART_BYTES, (name, off, per)
        offs[name] = (off, per)
        if not emit:
            return None
        h = nc.virtual_sbuf_tensor(name, list(shape), dtype)
        nc._tensor(h.name, list(h.shape), h.dtype, type="SB")
        ml = nc.lookup_mloc(h)
        ml.allocated = True
        ml.addr = base_addr[0] + off
        return h

    o = [0]

    def C(name, shape, dtype):
        per = int(np.prod(shape[1:])) * mybir.dt.size(dtype)
        h = carve(name, o[0], shape, dtype)
        o[0] += (per + 31) & ~31
        return h

    w3 = C("w3", [128, NK * 3 * D], BF16)
    u3 = C("u3", [128, NK * 3 * D], BF16)
    cfp = C("cfp", [128, NK * TOK], F32)
    cbf = C("cbf", [128, 2 * NK * TOK], BF16)
    xnb = C("xnb", [128, 2 * NK * TOK], BF16)
    gst = C("gst", [128, 3 * TOK], F32)
    xh2 = C("xh2", [128, 4 * D], F32)
    xs1 = C("xs1", [128, D], F32)
    xnh = C("xnh", [128, D], F32)
    sacc = C("sacc", [128, D], F32)
    ftmp = C("ftmp", [64, D], F32)
    rows = C("rows", [128, 2048], F32)
    xtk = C("xtk", [128, 2 * D], BF16)
    bstg = C("bstg", [128, D], BF16)
    vqct = C("vqct", [B, D], F32)
    gbf = C("gbf", [OM, B * OM], BF16)
    rotb = C("rotb", [OM, TOK], BF16)
    cb2 = C("cb2", [128, 2 * OM], F32)
    stat = C("stat", [128, 512], F32)
    smal = C("smal", [128, 256], F32)
    idt = C("idt", [128, 128], F32)
    onesml = C("onesml", [128, 128], F32)
    onesbf = C("onesbf", [1, 16], BF16)
    assert o[0] <= PART_BYTES, o[0]

    aoff = [offs["w3"][0]]

    def A(name, shape, dtype):
        per = int(np.prod(shape[1:])) * mybir.dt.size(dtype)
        h = carve(name, aoff[0], shape, dtype)
        aoff[0] += (per + 31) & ~31
        return h

    rotb2 = carve("rotb2", offs["gst"][0] + 4032, [OM, TOK], BF16)
    par_sb = carve("par_sb", offs["cfp"][0], [B, 4 * 504], F32)
    wo_a = A("wo_a", [128, NK * D], BF16)
    uo_a = A("uo_a", [128, NK * D], BF16)
    oow_a = A("oow_a", [OM, D], BF16)
    cu_a = A("cu_a", [128, NTC * D], F32)
    s_a = A("s_a", [128, NTC * D], F32)
    xt_a = A("xt_a", [128, D], F32)
    h_a = A("h_a", [128, D], F32)
    g_a = A("g_a", [128, D], F32)
    tmp_a = A("tmp_a", [128, 2 * D], F32)
    vqr_a = A("vqr_a", [128, D], F32)
    nwb = A("nwb", [128, D], F32)
    assert aoff[0] <= offs["u3"][0] + offs["u3"][1]

    xo = offs["xh2"][0]
    omu = carve("omu", xo, [OM, B, OM], F32)
    om2 = carve("om2", xo + 2048, [OM, B * OM], F32)
    opbr = carve("opbr", xo + 4096, [1, NPAR], BF16)
    pwt = carve("pwt", xo + 8160, [OM, D], BF16)
    vqw = carve("vqw", xo + 10208, [128, NK * OM], F32)
    gmat = carve("gmat", xo + 12256, [OM, B * OM], F32)
    cbn = carve("cbn", xo + 14304, [128, 2 * OM], F32)
    cbnT = carve("cbnT", xo + 14816, [OM, NCODE], F32)
    hmbf = carve("hmbf", xo + 15840, [128, 64], BF16)
    codbf = carve("codbf", xo + 15968, [OM, 32], BF16)
    opwb = carve("opwb", offs["gst"][0], [128, NPAR], BF16)
    xhb = carve("xhb", offs["sacc"][0], [128, 2048], BF16)
    rbf = carve("rbf", offs["stat"][0] + 64 * 4, [128, 64], BF16)
    xs1b = carve("xs1b", offs["ftmp"][0], [128, D], F32)

    NSC = 392

    if emit:
        for s in ["Y0", "W0", "W1", "N0", "N1", "S1", "S1B", "X0", "X1",
                  "XL0", "XL1", "XL2", "XL3", "XB0", "XB1",
                  "H0", "RB", "PW", "VW", "VQ", "ST0", "ST1", "GW", "GU",
                  "GB", "GP", "tE", "aE", "vE", "pE"]:
            sch.sems[s] = ctx.enter_context(nc.semaphore(s))
        block = ctx.enter_context(nc.Block())
        ps = [ctx.enter_context(nc.psum_tensor(f"psum{i}", [128, TOK], F32))
              for i in range(8)]
    else:
        ps = [None] * 8

    def rrow(level, c0, c1):
        return rows[level:level + 1, c0:c1]

    def tidx(t, k, g):
        return t * 24 + k * 3 + g

    slot_plan = {}
    _s1 = []
    for _ob in range(B):
        _s1 += [("p1t", _ob, _j) for _j in range(32)]
        _s1.append(("newt", _ob))
    hm_units = list(_s1[:60])
    _rest = _s1[60:]
    for _ob in range(B):
        _blk2 = [("p2t", _ob, _j) for _j in range(32)]
        _take1 = _rest[:33]
        _rest = _rest[33:]
        for _i in range(max(len(_blk2), len(_take1))):
            if _i < len(_blk2):
                hm_units.append(_blk2[_i])
            if _i < len(_take1):
                hm_units.append(_take1[_i])
    assert len(hm_units) == 520 and not _rest, (len(hm_units), len(_rest))
    _nxt0 = [1]
    _slots0 = []
    for _t in range(B):
        for _k in range(NK):
            if _nxt0[0] < B and _k % 3 == 2:
                _slots0.append(("own", _nxt0[0]))
                _nxt0[0] += 1
            else:
                _slots0.append("u")
    _own_upos = {}
    _uc = 0
    for _s in _slots0:
        if _s == "u":
            _uc += 10
        else:
            _own_upos[_s[1]] = _uc
    for _b in sorted(_own_upos, reverse=True):
        _pos = max(0, _own_upos[_b] - 15 - 5 * (_b - 1))
        ins = [("o1t", _b, _i) for _i in range(4)] + [("onewt", _b)]
        hm_units[_pos:_pos] = ins
    tile_seq = {}
    tile_list = []
    for _u in hm_units:
        if _u[0] in ("p1t", "o1t"):
            tile_seq[(_u[0], _u[1], _u[2])] = len(tile_list)
            tile_list.append(_u)
    _ui = [0]
    _nxt = [1]
    for _t in range(B):
        for _k in range(NK):
            if _nxt[0] < B and _k % 3 == 2:
                slot_plan[(_t, _k)] = ("own", _nxt[0])
                _nxt[0] += 1
            else:
                take = hm_units[_ui[0]:_ui[0] + 10]
                _ui[0] += len(take)
                slot_plan[(_t, _k)] = ("units", take)
    hm_tail = hm_units[_ui[0]:]

    def tile_done_label(u):
        if u[0] == "p1t":
            return f"a:hssq:{u[1]}:{u[2]}"
        return f"a:ssqo:{u[1]}:{u[2]}"

    def stream_tensor(eng=None):
        sch.stream("tE", eng)
        T = sch

        T.w("y:rows")
        T.w("v:ones")
        for k in range(NK):
            if k >= 2:
                T.w(f"v:c0cp:{k - 2}")
            T.op(lambda e, k=k: e.matmul(
                ps[6 + k % 2][:, :],
                rrow(0, 1024 + k * 128, 1024 + (k + 1) * 128),
                rrow(0, 0, 512), start=True, stop=True))
            T.mark(f"t:c0:{k}")
        T.w("y:b3")
        T.w("y:idt")
        T.w("v:c0cp:7")
        for j in range(24):
            T.op(lambda e, j=j: e.transpose(
                ps[7][0:128, j:j + 1], xh2[0:1, j * 128:(j + 1) * 128],
                idt[0:1, 0:1]), inc=(j == 23))
        T.mark("t:b3t")

        def hs_mm(ob, j):
            if j == 0:
                if ob == 0:
                    T.w("v:c0cp:7")
                    T.w("v:b3c")
                else:
                    T.w(f"v:hsevac:{ob - 1}")
                T.w(f"v:rbf:{ob}")
            T.w(f"g:xb:{ob}:{j}")
            rcol = (ob % 2) * 32 + j
            sl3 = (ob * 32 + j) % 3
            rh = (xhb[:, sl3 * 1024:(sl3 + 1) * 1024] if sl3 < 2
                  else bstg[:, 0:1024]) if emit else None
            T.op(lambda e, rcol=rcol, rh=rh, j=j: e.matmul(
                ps[6][0:1, 0:512], rbf[:, rcol:rcol + 1], rh[:, 0:512],
                start=(j == 0), stop=(j == 31), skip_group_check=True),
                inc=False)
            T.op(lambda e, rcol=rcol, rh=rh, j=j: e.matmul(
                ps[7][0:1, 0:512], rbf[:, rcol:rcol + 1], rh[:, 512:1024],
                start=(j == 0), stop=(j == 31), skip_group_check=True))
            T.mark(f"t:hs:{ob}:{j}")

        for t in range(B):
            par = t % 2
            for k in range(NK):
                for g in range(3):
                    ti = tidx(t, k, g)
                    bank = ti % 6
                    col = g * D + k * 128
                    if ti >= 6:
                        pt = (ti - 6) // 24
                        pk = ((ti - 6) % 24) // 3
                        pg = (ti - 6) % 3
                        if pg == 0:
                            T.w(f"v:cup:{pt}:{pk}")
                        else:
                            T.w(f"a:ev:{pt}:{pk}:{pg}")
                    T.w(f"y:xnb:{t}")
                    for kk in range(NK):
                        T.w(f"g:w3:{kk}")
                        T.op(lambda e, bank=bank, kk=kk, col=col, t=t:
                             e.matmul(
                                 ps[bank][:, :],
                                 w3[:, kk * 3072 + col:kk * 3072 + col + 128],
                                 xnb[:, (t % 2) * 4096 + kk * 512:
                                     (t % 2) * 4096 + (kk + 1) * 512],
                                 start=(kk == 0), stop=False), inc=False)
                    for kk in range(NK):
                        T.w(f"g:u3:{kk}")
                        if t == 0:
                            T.w(f"v:c0cp:{kk}")
                        else:
                            T.w(f"v:cup:{t - 1}:{kk}")
                        T.op(lambda e, bank=bank, kk=kk, col=col, par=par:
                             e.matmul(
                                 ps[bank][:, :],
                                 u3[:, kk * 3072 + col:kk * 3072 + col + 128],
                                 cbf[:, par * 4096 + kk * 512:
                                     par * 4096 + (kk + 1) * 512],
                                 start=False, stop=(kk == NK - 1)),
                             inc=(kk == NK - 1))
                    T.mark(f"t:mm:{t}:{k}:{g}")
                kind = slot_plan[(t, k)]
                if kind[0] == "units":
                    for u in kind[1]:
                        if u[0] == "p2t":
                            hs_mm(u[1], u[2])
            T.mark(f"t:step:{t}")

        for u in hm_tail:
            if u[0] == "p2t":
                hs_mm(u[1], u[2])
        if PHASE < 2 or (PHASE == 2 and SUB2 < 2):
            return
        T.w("a:ev:7:7:2")
        T.w("v:cup:7:7")
        for bidx in range(2 * HB):
            T.w(f"y:hmrow:{bidx}")
            if bidx > 0:
                T.w(f"v:hmb:{bidx - 1}")
            for kk in range(NK):
                T.op(lambda e, kk=kk: e.transpose(
                    ps[1][0:128, kk:kk + 1], xnh[0:1, kk * 128:(kk + 1) * 128],
                    idt[0:1, 0:1]), inc=(kk == NK - 1))
            T.mark(f"t:hmT:{bidx}")
        if PHASE < 3:
            return
        T.w("y:mix")
        T.w("v:hsevac:7")
        T.op(lambda e: e.matmul(ps[7][:, 0:2], rrow(0, 0, 128),
                                smal[0:1, 0:2], start=True, stop=True))
        T.mark("t:mixb")
        T.w("v:mixcp")
        for kk in range(NK):
            T.w(f"g:opw:{kk}")
            T.w("v:hmbf")
            for q in range(4):
                T.op(lambda e, kk=kk, q=q: e.matmul(
                    ps[2 + q][0:B, 0:504], hmbf[:, kk * 8:(kk + 1) * 8],
                    opwb[:, q * 504:(q + 1) * 504],
                    start=(kk == 0), stop=False, skip_group_check=True),
                    inc=(q == 3))
            T.mark(f"t:parchunk:{kk}")
        T.w("y:opb")
        for q in range(4):
            T.op(lambda e, q=q: e.matmul(
                ps[2 + q][0:B, 0:504], onesbf[0:1, 0:8],
                opbr[0:1, q * 504:(q + 1) * 504], start=False, stop=True,
                skip_group_check=True))
            T.mark(f"t:par:{q}")
        for b in range(B):
            if b == 0:
                for _q in ["XL0", "XL1", "XL2", "XL3"]:
                    T.w(f"g:scat:{_q}")
            T.w("v:mixcp" if b == 0 else f"v:gbf:{b - 1}")
            T.op(lambda e, b=b: e.transpose(
                ps[7][0:OM, 0:OM], omu[0:OM, b, :], idt[0:OM, 0:OM]))
            T.mark(f"t:omT:{b}")
            T.w(f"v:omfull:{b}")
            T.op(lambda e, b=b: e.matmul(
                ps[6][0:OM, 0:OM], om2[0:OM, b * OM:(b + 1) * OM],
                omu[0:OM, b, :], start=True, stop=True))
            T.mark(f"t:om2:{b}")
            T.w(f"v:om2cp:{b}")
            T.op(lambda e, b=b: e.matmul(
                ps[7][0:OM, 0:OM], om2[0:OM, b * OM:(b + 1) * OM],
                omu[0:OM, b, :], start=True, stop=True))
            T.mark(f"t:om3:{b}")
        T.w("v:hm")
        T.w("y:vqw")
        for kk in range(NK):
            T.op(lambda e, kk=kk: e.matmul(
                ps[0][0:B, 0:OM], smal[:, 128 + kk * 8:128 + (kk + 1) * 8],
                vqw[:, kk * OM:(kk + 1) * OM],
                start=(kk == 0), stop=False), inc=False)
        T.op(lambda e: e.matmul(ps[0][0:B, 0:OM], rrow(0, 0, 8),
                                rrow(0, 512, 512 + OM),
                                start=False, stop=True))
        T.mark("t:vqin")
        T.w("v:cbn")
        T.w("v:hmb:7")
        for j in range(2):
            if j > 0:
                T.w("v:cbnTcp:0")
            T.op(lambda e, j=j: e.transpose(
                ps[1][0:OM, 0:128], cbn[:, j * OM:(j + 1) * OM],
                idt[0:128, 0:128]))
            T.mark(f"t:cbnT:{j}")
        T.w("v:vqi")
        T.w("v:pst:0")
        T.op(lambda e: e.transpose(ps[2][0:OM, 0:B], smal[0:B, 32:96],
                                   idt[0:B, 0:B]))
        T.mark("t:vqiT")
        T.w("v:vqiTcp")
        T.w("v:cbnTcp:1")
        T.w("v:pst:1")
        T.op(lambda e: e.matmul(ps[3][0:B, 0:NCODE], smal[0:OM, 104:112],
                                cbnT[:, :], start=True, stop=True))
        T.mark("t:scores")
        T.w("v:oh")
        T.w("v:pst:2")
        for j in range(2):
            T.op(lambda e, j=j: e.transpose(
                ps[4][0:128, j * 8:(j + 1) * 8],
                stat[0:B, 128 + j * 128:128 + (j + 1) * 128],
                idt[0:B, 0:B]), inc=(j == 1))
        T.mark("t:ohT")
        T.w("v:ohTcp")
        T.w("v:pst:3")
        for j in range(2):
            T.op(lambda e, j=j: e.matmul(
                ps[5][0:OM, 0:B], cb2[:, j * OM:(j + 1) * OM],
                smal[:, 112 + j * 8:112 + (j + 1) * 8],
                start=(j == 0), stop=(j == 1)), inc=(j == 1))
        T.mark("t:code")
        T.w("v:codecp")
        T.w("y:pwt")
        for osl in range(2):
            T.op(lambda e, osl=osl: e.matmul(
                ps[6][0:B, 0:512], codbf[:, 0:8],
                pwt[:, osl * 512:(osl + 1) * 512],
                start=True, stop=False), inc=False)
            T.op(lambda e, osl=osl: e.matmul(
                ps[6][0:B, 0:512], onesml[64:65, 0:8],
                rrow(64, D + osl * 512, D + (osl + 1) * 512),
                start=False, stop=True))
            T.mark(f"t:vqct:{osl}")
            T.w(f"v:vqctcp:{osl}")
        T.w("v:gbf:7")
        for osl in range(2):
            T.op(lambda e, osl=osl: e.matmul(
                ps[7][:, :], onesml[32:33, 0:128],
                rrow(32, D + osl * 512, D + (osl + 1) * 512),
                start=True, stop=True))
            T.mark(f"t:nwb:{osl}")
            T.w(f"v:nwbcp:{osl}")

        if PHASE < 4 or SUB4 < 1:
            return
        seq = [0]

        def bankwait(i):
            if i >= 6:
                T.w(f"v:bfree:{i - 6}")

        fpar = B % 2
        for tc in range(NTC):
            for osl in range(2):
                i = seq[0]
                bank = i % 6
                bankwait(i)
                T.w("g:uoa")
                for kk in range(NK):
                    T.op(lambda e, bank=bank, kk=kk, tc=tc, osl=osl:
                         e.matmul(
                             ps[bank][:, :],
                             cbf[:, fpar * 4096 + kk * 512 + tc * 128:
                                 fpar * 4096 + kk * 512 + tc * 128 + 128],
                             uo_a[:, kk * D + osl * 512:
                                  kk * D + (osl + 1) * 512],
                             start=(kk == 0), stop=False), inc=False)
                T.op(lambda e, bank=bank, osl=osl: e.matmul(
                    ps[bank][:, :], onesml[32:33, 0:128],
                    rrow(32, osl * 512, (osl + 1) * 512),
                    start=False, stop=True))
                T.mark(f"t:cu:{i}")
                seq[0] += 1

        if SUB4 < 4:
            return
        def rot_mm(t):
            obuf = t % 2
            T.w(f"y:xnbo:{t}")
            T.w(f"v:gbf:{t}")
            if t == 0:
                T.w("v:vqctcp:1")
            elif t == 1:
                T.w("v:nwbcp:1")
            else:
                T.w(f"v:rotbf:{t - 2}")
            T.op(lambda e, t=t, obuf=obuf: e.matmul(
                ps[6 + t % 2][0:OM, :], gbf[:, t * OM:(t + 1) * OM],
                xnb[0:OM, obuf * 4096:obuf * 4096 + 512],
                start=True, stop=True))
            T.mark(f"t:rot:{t}")

        for t in range(B):
            obuf = t % 2
            if t == 0:
                rot_mm(0)
            T.w(f"v:rotbf:{t}")
            if SUB4 == 4 and SUB4R < 1:
                continue
            rslot = rotb if t % 2 == 0 else rotb2
            for tc in range(NTC):
                for osl in range(2):
                    i = seq[0]
                    bank = i % 6
                    bankwait(i)
                    T.w("g:oowa")
                    T.w(f"v:vqo:{t}")
                    _m = [("rot",), ("vqr",)]
                    for _j, _mm in enumerate(_m):
                        _st = (_j == 0)
                        _sp = (_j == len(_m) - 1)
                        if _mm[0] == "rot":
                            T.op(lambda e, bank=bank, tc=tc, osl=osl,
                                 rslot=rslot, _st=_st, _sp=_sp: e.matmul(
                                     ps[bank][:, :],
                                     rslot[:, tc * 128:(tc + 1) * 128],
                                     oow_a[:, osl * 512:(osl + 1) * 512],
                                     start=_st, stop=_sp), inc=_sp)
                        else:
                            T.op(lambda e, bank=bank, osl=osl, _st=_st,
                                 _sp=_sp: e.matmul(
                                     ps[bank][:, :], onesml[64:65, 0:128],
                                     vqr_a[64:65, osl * 512:(osl + 1) * 512],
                                     start=_st, stop=_sp), inc=_sp)
                    T.mark(f"t:sp:{t}:{tc}:{osl}")
                    seq[0] += 1
            if t + 1 < B:
                rot_mm(t + 1)
            if SUB4 < 5:
                continue
            for tc in range(NTC):
                for osl in range(2):
                    i = seq[0]
                    bank = i % 6
                    bankwait(i)
                    T.w("g:woa")
                    for kk in range(NK):
                        T.op(lambda e, bank=bank, kk=kk, tc=tc, osl=osl,
                             obuf=obuf: e.matmul(
                                 ps[bank][:, :],
                                 xnb[:, obuf * 4096 + kk * 512 + tc * 128:
                                     obuf * 4096 + kk * 512 + tc * 128 + 128],
                                 wo_a[:, kk * D + osl * 512:
                                      kk * D + (osl + 1) * 512],
                                 start=(kk == 0), stop=(kk == NK - 1)),
                             inc=(kk == NK - 1))
                    T.mark(f"t:h:{t}:{tc}:{osl}")
                    seq[0] += 1
        T.mark("t:done")

    def stream_scalar(eng=None):
        sch.stream("aE", eng)
        A_ = sch

        def own_b(b):
            if b == 0:
                for tc in range(NTC):
                    if tc >= 2:
                        A_.w(f"a:ssq:{b}:{tc - 2}")
                    dstb = xs1 if tc % 2 == 0 else xs1b
                    qn = "S1" if tc % 2 == 0 else "S1B"
                    A_.dma(qn, lambda e, b=b, tc=tc, dstb=dstb: e.dma_start(
                        out=dstb[:, :],
                        in_=io["XS"][b, tc * 128:(tc + 1) * 128, :]))
                    sch.mark(f"q:xs:{b}:{tc}", qn)
                    A_.w(f"v:xsum:{b}:{tc}")
                    A_.op(lambda e, tc=tc, dstb=dstb: e.activation(
                        dstb[:, :], dstb[:, :], AF.Square,
                        accum_out=stat[:, 384 + tc:384 + tc + 1]))
                    A_.mark(f"a:ssq:{b}:{tc}")
            def scw(tc):
                A_.w(f"v:xntok:{b}:{tc}")
                A_.dma(f"W{tc % 2}", lambda e, b=b, tc=tc: e.dma_start(
                    out=io["xn_sc"][b, tc * 128:(tc + 1) * 128, :],
                    in_=xtk[:, (tc % 2) * D:(tc % 2 + 1) * D]))
                sch.mark(f"y:scw:{b}:{tc}", f"W{tc % 2}")

            for tc in range(NTC):
                if tc >= 2:
                    A_.w(f"v:xntok:{b}:{tc - 2}")
                elif b > 0:
                    A_.w(f"v:xntok:{b - 1}:{tc + 2}")
                dst = xs1 if tc % 2 == 0 else xs1b
                qn = "S1" if tc % 2 == 0 else "S1B"
                A_.dma(qn, lambda e, b=b, tc=tc, dst=dst: e.dma_start(
                    out=dst[:, :],
                    in_=io["XS"][b, tc * 128:(tc + 1) * 128, :]))
                sch.mark(f"q:xsr:{b}:{tc}", qn)
                if tc >= 1:
                    scw(tc - 1)
            scw(NTC - 1)
            sch.mark(f"y:scwb0:{b}", "W0")
            sch.mark(f"y:scwb1:{b}", "W1")

        def do_unit_act(u):
            if u[0] == "p1t":
                ob, j = u[1], u[2]
                oc = (ob % 4) * 32
                A_.w(f"v:hsum:{ob}:{j}")
                b4 = tile_seq[(u[0], ob, j)] % 4
                A_.op(lambda e, b4=b4, oc=oc, j=j: e.activation(
                    xh2[:, b4 * D:(b4 + 1) * D], xh2[:, b4 * D:(b4 + 1) * D],
                    AF.Square,
                    accum_out=stat[:, 256 + oc + j:256 + oc + j + 1]))
                A_.mark(f"a:hssq:{ob}:{j}")
            elif u[0] == "o1t":
                b, i = u[1], u[2]
                col = 96 + 8 * (b % 2) + 4 + i
                A_.w(f"v:hsumo:{b}:{i}")
                b4 = tile_seq[(u[0], b, i)] % 4
                A_.op(lambda e, b4=b4, col=col: e.activation(
                    xh2[:, b4 * D:(b4 + 1) * D], xh2[:, b4 * D:(b4 + 1) * D],
                    AF.Square, accum_out=stat[:, col:col + 1]))
                A_.mark(f"a:ssqo:{b}:{i}")

        own_b(0)
        for t in range(B):
            for k in range(NK):
                for g in range(3):
                    A_.w(f"t:mm:{t}:{k}:{g}")
                    if g == 0:
                        if k > 0:
                            A_.w(f"v:cup:{t}:{k - 1}")
                        elif t > 0:
                            A_.w(f"v:cup:{t - 1}:{NK - 1}")
                    func = AF.Sigmoid if g < 2 else AF.Tanh
                    A_.op(lambda e, t=t, k=k, g=g, func=func: e.activation(
                        gst[:, g * TOK:(g + 1) * TOK],
                        ps[tidx(t, k, g) % 6][:, :], func,
                        bias=smal[:, 8 + g * 8 + k:8 + g * 8 + k + 1]))
                    A_.mark(f"a:ev:{t}:{k}:{g}")
                kind = slot_plan[(t, k)]
                if kind[0] == "own":
                    own_b(kind[1])
                else:
                    for u in kind[1]:
                        do_unit_act(u)
        for u in hm_tail:
            do_unit_act(u)

        if PHASE < 3:
            return
        A_.w("v:hsum:7:31")
        A_.w("y:cb")
        for j in range(2):
            A_.op(lambda e, j=j: e.activation(
                cbn[:, j * OM:(j + 1) * OM], cb2[:, j * OM:(j + 1) * OM],
                AF.Square, accum_out=smal[:, 200 + j:200 + j + 1]))
        A_.mark("a:cbsq")

        if PHASE < 4 or SUB4 < 3:
            return
        for t in range(B):
            if t == 0:
                A_.w("t:om3:7")
                A_.w("t:vqin")
                A_.w("t:vqct:1")
                A_.w("t:par:3")
                A_.w("v:gbf:7")
            if t > 0 and SUB4 >= 5:
                A_.w(f"p:out:{t - 1}:1")
            h0 = A_.dma("X0", lambda e, t=t: e.dma_start(
                out=xh2[:, 0:2 * D],
                in_=io["XS"][t, 0:256, :].rearrange("(c p) d -> p c d",
                                                    p=128)))
            if t > 0 and SUB4 >= 5:
                A_.w(f"p:out:{t - 1}:3")
            h1 = A_.dma("X1", lambda e, t=t: e.dma_start(
                out=xh2[:, 2 * D:4 * D],
                in_=io["XS"][t, 256:512, :].rearrange("(c p) d -> p c d",
                                                      p=128)))
            sch.labels[f"q:xth:{t}:0"] = ("X0", h0)
            sch.labels[f"q:xth:{t}:1"] = ("X1", h1)
            if SUB4 < 4 or (SUB4 == 4 and SUB4R < 2):
                continue
            for tc in range(NTC):
                A_.w(f"v:spadd:{t}:{tc}")
                A_.op(lambda e, tc=tc: e.activation(
                    xnh[:, 0:D], s_a[:, tc * D:(tc + 1) * D], AF.Square,
                    accum_out=stat[:, 496 + tc:496 + tc + 1]))
                A_.mark(f"a:ms:{t}:{tc}")
            if SUB4 < 5:
                continue
            for tc in range(NTC):
                for osl in range(2):
                    if osl == 0 and (t > 0 or tc > 0):
                        _pv = (t, tc - 1) if tc > 0 else (t - 1, NTC - 1)
                        A_.w(f"p:tm:{_pv[0]}:{_pv[1]}")
                    A_.w(f"v:hpre:{t}:{tc}:{osl}")
                    A_.op(lambda e, osl=osl: e.activation(
                        h_a[:, osl * 512:(osl + 1) * 512],
                        h_a[:, osl * 512:(osl + 1) * 512], AF.Tanh),
                        inc=False)
                    A_.op(lambda e, osl=osl: e.activation(
                        g_a[:, osl * 512:(osl + 1) * 512],
                        h_a[:, osl * 512:(osl + 1) * 512], AF.Sigmoid))
                    A_.mark(f"a:hg:{t}:{tc}:{osl}")
        A_.mark("a:done")

    def stream_vector(eng=None):
        sch.stream("vE", eng)
        V = sch

        def newton(dst_r, dst_negrm, src_sum, src_ssq, width, eps):
            mcol = (stat[:, NSC:NSC + width] if emit else None)
            v2 = (stat[:, NSC + 32:NSC + 32 + width] if emit else None)
            y = (stat[:, NSC + 64:NSC + 64 + width] if emit else None)
            V.op(lambda e: e.tensor_scalar(mcol, src_sum, 1.0 / D, None,
                                           ALU.mult), inc=False)
            V.op(lambda e: e.tensor_scalar(v2, src_ssq, 1.0 / D, None,
                                           ALU.mult), inc=False)
            V.op(lambda e: e.scalar_tensor_tensor(
                y, mcol, -1.0, mcol, ALU.mult, ALU.mult), inc=False)
            V.op(lambda e: e.tensor_add(v2, v2, y), inc=False)
            V.op(lambda e: e.tensor_scalar(v2, v2, eps, None, ALU.add),
                 inc=False)
            V.op(lambda e: e.tensor_scalar(y, v2, -0.5, 1.5, ALU.mult,
                                           ALU.add), inc=False)
            for _ in range(NEWT):
                V.op(lambda e: e.scalar_tensor_tensor(
                    dst_r, y, 1.0, y, ALU.mult, ALU.mult), inc=False)
                V.op(lambda e: e.scalar_tensor_tensor(
                    dst_r, dst_r, -0.5, v2, ALU.mult, ALU.mult), inc=False)
                V.op(lambda e: e.scalar_tensor_tensor(
                    y, dst_r, 1.5, y, ALU.add, ALU.mult), inc=False)
            V.op(lambda e: e.tensor_copy(dst_r, y), inc=False)
            V.op(lambda e: e.scalar_tensor_tensor(
                dst_negrm, y, -1.0, mcol, ALU.mult, ALU.mult))

        V.op(lambda e: e.memset(rows[0:1, 0:512], 1.0), inc=False)
        V.op(lambda e: e.memset(onesml[:, :], 1.0), inc=False)
        V.op(lambda e: e.memset(onesbf[0:1, :], 1.0))
        V.mark("v:ones")
        for k in range(NK):
            V.w(f"t:c0:{k}")
            V.op(lambda e, k=k: e.tensor_copy(
                cfp[:, k * 512:(k + 1) * 512], ps[6 + k % 2][:, :]),
                inc=False)
            V.op(lambda e, k=k: e.tensor_copy(
                cbf[:, k * 512:(k + 1) * 512], ps[6 + k % 2][:, :]))
            V.mark(f"v:c0cp:{k}")
        V.w("t:b3t")
        V.op(lambda e: e.tensor_copy(smal[:, 8:32], ps[7][0:128, 0:24]))
        V.mark("v:b3c")

        vunits = []

        def own_newt(b):
            V.w(f"a:ssq:{b}:{NTC - 1}")
            newton(stat[:, 32 + b * 4:32 + b * 4 + 4] if emit else None,
                   stat[:, b * 4:b * 4 + 4] if emit else None,
                   stat[:, 388:392] if emit else None,
                   stat[:, 384:388] if emit else None, 4, LN_EPS)
            V.mark(f"v:newt:own:{b}")

        def hm_newt(ob):
            oc = (ob % 4) * 32
            V.w(f"a:hssq:{ob}:31")
            newton(stat[:, 256 + oc:256 + oc + 32] if emit else None,
                   stat[:, 128 + oc:128 + oc + 32] if emit else None,
                   stat[:, 128 + oc:128 + oc + 32] if emit else None,
                   stat[:, 256 + oc:256 + oc + 32] if emit else None,
                   32, LN_EPS)
            if ob >= 2:
                V.w(f"t:hs:{ob - 2}:31")
            V.op(lambda e, ob=ob, oc=oc: e.tensor_copy(
                rbf[:, (ob % 2) * 32:(ob % 2) * 32 + 32],
                stat[:, 256 + oc:256 + oc + 32]), inc=False)
            if ob > 0:
                V.w(f"p:negs:{ob - 1}")
            V.op(lambda e, oc=oc: e.tensor_reduce(
                stat[:, 488:489], stat[:, 128 + oc:128 + oc + 32],
                axis=AX.X, op=ALU.add))
            V.mark(f"v:negsum:{ob}")
            V.mark(f"v:rbf:{ob}")

        def own_dve(b):
            if b == 0:
                for tc in range(NTC):
                    V.w(f"q:xs:{b}:{tc}")
                    srcb = xs1 if tc % 2 == 0 else xs1b
                    V.op(lambda e, tc=tc, srcb=srcb: e.tensor_reduce(
                        stat[:, 388 + tc:388 + tc + 1], srcb[:, :], axis=AX.X,
                        op=ALU.add))
                    V.mark(f"v:xsum:{b}:{tc}")
                own_newt(b)
            for tc in range(NTC):
                if tc >= 2:
                    V.w(f"y:scw:{b}:{tc - 2}")
                elif b > 0:
                    V.w(f"y:scw:{b - 1}:{tc + 2}")
                V.w(f"q:xsr:{b}:{tc}")
                srcb = xs1 if tc % 2 == 0 else xs1b
                V.op(lambda e, b=b, tc=tc, srcb=srcb: e.tensor_scalar(
                    xtk[:, (tc % 2) * D:(tc % 2 + 1) * D], srcb[:, :],
                    stat[:, 32 + b * 4 + tc:32 + b * 4 + tc + 1],
                    stat[:, b * 4 + tc:b * 4 + tc + 1],
                    ALU.mult, ALU.add))
                V.mark(f"v:xntok:{b}:{tc}")

        def hsevac(ob):
            V.w(f"t:hs:{ob}:31")
            V.w(f"p:negs:{ob}")
            if ob > 0:
                V.w(f"y:shd:{ob - 1}")
            V.op(lambda e: e.tensor_scalar(
                xnh[0:1, 0:512], ps[6][0:1, 0:512], stat[0:1, 489:490],
                None, ALU.add), inc=False)
            V.op(lambda e: e.tensor_scalar(
                xnh[0:1, 512:1024], ps[7][0:1, 0:512], stat[0:1, 489:490],
                None, ALU.add))
            V.mark(f"v:hsevac:{ob}")

        def do_unit_dve(u):
            if u[0] == "p1t":
                ob, j = u[1], u[2]
                oc = (ob % 4) * 32
                b4 = tile_seq[(u[0], ob, j)] % 4
                V.w(f"q:x1:{ob}:{j}")
                V.op(lambda e, b4=b4, oc=oc, j=j: e.tensor_reduce(
                    stat[:, 128 + oc + j:128 + oc + j + 1],
                    xh2[:, b4 * D:(b4 + 1) * D], axis=AX.X, op=ALU.add))
                V.mark(f"v:hsum:{u[1]}:{u[2]}")
            elif u[0] == "o1t":
                b, i = u[1], u[2]
                col = 96 + 8 * (b % 2) + i
                b4 = tile_seq[(u[0], b, i)] % 4
                V.w(f"q:xo:{b}:{i}")
                V.op(lambda e, b4=b4, col=col: e.tensor_reduce(
                    stat[:, col:col + 1], xh2[:, b4 * D:(b4 + 1) * D],
                    axis=AX.X, op=ALU.add))
                V.mark(f"v:hsumo:{b}:{i}")
            elif u[0] == "onewt":
                b = u[1]
                co = 96 + 8 * (b % 2)
                V.w(f"a:ssqo:{b}:3")
                newton(stat[:, 32 + b * 4:32 + b * 4 + 4] if emit else None,
                       stat[:, b * 4:b * 4 + 4] if emit else None,
                       stat[:, co:co + 4] if emit else None,
                       stat[:, co + 4:co + 8] if emit else None, 4, LN_EPS)
                V.mark(f"v:newt:own:{b}")
            elif u[0] == "newt":
                hm_newt(u[1])
            elif u[0] == "p2t" and u[2] == 31:
                hsevac(u[1])

        own_dve(0)

        for t in range(B):
            par = t % 2
            npar = 1 - par
            for k in range(NK):
                fb = tidx(t, k, 0) % 6
                V.w(f"a:ev:{t}:{k}:2")
                V.op(lambda e, fb=fb, k=k: e.tensor_mul(
                    ps[fb][:, :], gst[:, 0:TOK],
                    cfp[:, k * 512:(k + 1) * 512]), inc=False)
                V.op(lambda e, k=k: e.scalar_tensor_tensor(
                    cfp[:, k * 512:(k + 1) * 512],
                    gst[:, TOK:2 * TOK], 1.0,
                    gst[:, 2 * TOK:3 * TOK], ALU.mult, ALU.mult), inc=False)
                V.op(lambda e, k=k, fb=fb: e.tensor_add(
                    cfp[:, k * 512:(k + 1) * 512],
                    cfp[:, k * 512:(k + 1) * 512], ps[fb][:, :]), inc=False)
                V.op(lambda e, k=k, npar=npar: e.tensor_copy(
                    cbf[:, npar * 4096 + k * 512:npar * 4096 + (k + 1) * 512],
                    cfp[:, k * 512:(k + 1) * 512]))
                V.mark(f"v:cup:{t}:{k}")
                kind = slot_plan[(t, k)]
                if kind[0] == "own":
                    own_dve(kind[1])
                else:
                    for u in kind[1]:
                        do_unit_dve(u)
        for u in hm_tail:
            do_unit_dve(u)

        if PHASE < 2 or (PHASE == 2 and SUB2 < 2):
            return
        if PHASE >= 3:
            V.w("a:hssq:7:31")
            V.op(lambda e: e.memset(omu[:, :, :], 0.0))
            V.mark("v:omz")

        for bidx in range(2 * HB):
            V.w(f"t:hmT:{bidx}")
            for kk in range(NK):
                V.op(lambda e, bidx=bidx, kk=kk: e.tensor_scalar(
                    smal[:, 128 + kk * 8 + bidx:128 + kk * 8 + bidx + 1],
                    ps[1][0:128, kk:kk + 1], 1.0 / L, None, ALU.mult),
                    inc=(kk == NK - 1))
            V.mark(f"v:hmb:{bidx}")
        V.mark("v:hm")
        V.op(lambda e: e.tensor_copy(hmbf[:, :], smal[:, 128:192]))
        V.mark("v:hmbf")
        if PHASE < 3:
            return
        V.w("t:mixb")
        V.op(lambda e: e.tensor_copy(smal[:, 0:2], ps[7][:, 0:2]))
        V.mark("v:mixcp")
        V.op(lambda e: e.tensor_scalar(
            rows[64:65, 0:D], rows[64:65, 0:D], smal[64:65, 0:1], None,
            ALU.mult))
        V.mark("v:oobs")
        for q in range(4):
            V.w(f"t:par:{q}")
            V.op(lambda e, q=q: e.tensor_copy(
                par_sb[0:B, q * 504:(q + 1) * 504], ps[2 + q][0:B, 0:504]))
            V.mark(f"v:pst:{q}")
        for b in range(B):
            V.w(f"t:omT:{b}")
            V.op(lambda e, b=b: e.tensor_sub(
                omu[0:OM, b, :], omu[0:OM, b, :], ps[7][0:OM, 0:OM]),
                inc=False)
            V.op(lambda e, b=b: e.tensor_scalar(
                om2[0:OM, b * OM:(b + 1) * OM], omu[0:OM, b, :], -1.0, None,
                ALU.mult))
            V.mark(f"v:omfull:{b}")
            V.w(f"t:om2:{b}")
            V.op(lambda e, b=b: e.tensor_copy(
                om2[0:OM, b * OM:(b + 1) * OM], ps[6][0:OM, 0:OM]))
            V.mark(f"v:om2cp:{b}")
            V.w(f"t:om3:{b}")
            V.op(lambda e, b=b: e.scalar_tensor_tensor(
                gmat[0:OM, b * OM:(b + 1) * OM],
                om2[0:OM, b * OM:(b + 1) * OM], 0.5,
                omu[0:OM, b, :], ALU.mult, ALU.add), inc=False)
            V.op(lambda e, b=b: e.scalar_tensor_tensor(
                gmat[0:OM, b * OM:(b + 1) * OM], ps[7][0:OM, 0:OM],
                1.0 / 6.0, gmat[0:OM, b * OM:(b + 1) * OM],
                ALU.mult, ALU.add), inc=False)
            V.op(lambda e, b=b: e.tensor_add(
                gmat[0:OM, b * OM:(b + 1) * OM],
                gmat[0:OM, b * OM:(b + 1) * OM], idt[0:OM, 0:OM]),
                inc=False)
            V.op(lambda e, b=b: e.tensor_copy(
                gbf[:, b * OM:(b + 1) * OM], gmat[0:OM, b * OM:(b + 1) * OM]))
            V.mark(f"v:gbf:{b}")
        V.w("a:cbsq")
        sc = smal
        V.op(lambda e: e.tensor_scalar(sc[:, 202:204], sc[:, 200:202],
                                       1.0 / OM, None, ALU.mult), inc=False)
        V.op(lambda e: e.tensor_scalar(sc[:, 204:206], sc[:, 202:204],
                                       -0.5, 1.5, ALU.mult, ALU.add),
             inc=False)
        for _ in range(NEWT):
            V.op(lambda e: e.scalar_tensor_tensor(
                sc[:, 206:208], sc[:, 204:206], 1.0, sc[:, 204:206],
                ALU.mult, ALU.mult), inc=False)
            V.op(lambda e: e.scalar_tensor_tensor(
                sc[:, 206:208], sc[:, 206:208], -0.5, sc[:, 202:204],
                ALU.mult, ALU.mult), inc=False)
            V.op(lambda e: e.scalar_tensor_tensor(
                sc[:, 204:206], sc[:, 206:208], 1.5, sc[:, 204:206],
                ALU.add, ALU.mult), inc=False)
        V.op(lambda e: e.tensor_scalar(sc[:, 204:206], sc[:, 204:206],
                                       0.125, None, ALU.mult), inc=False)
        for j in range(2):
            V.op(lambda e, j=j: e.tensor_scalar(
                cbn[:, j * OM:(j + 1) * OM], cb2[:, j * OM:(j + 1) * OM],
                sc[:, 204 + j:205 + j], None, ALU.mult), inc=(j == 1))
        V.mark("v:cbn")
        V.w("t:vqin")
        V.op(lambda e: e.tensor_copy(smal[0:B, 32:96], ps[0][0:B, 0:OM]))
        V.mark("v:vqi")
        for j in range(2):
            V.w(f"t:cbnT:{j}")
            V.op(lambda e, j=j: e.tensor_copy(
                cbnT[:, j * 128:(j + 1) * 128], ps[1][0:OM, 0:128]))
            V.mark(f"v:cbnTcp:{j}")
        V.w("t:vqiT")
        V.op(lambda e: e.tensor_copy(smal[0:OM, 104:112], ps[2][0:OM, 0:B]))
        V.mark("v:vqiTcp")
        V.w("t:scores")
        V.op(lambda e: e.tensor_copy(stat[0:B, 128:384],
                                     ps[3][0:B, 0:NCODE]), inc=False)
        V.op(lambda e: e.reduce_max(smal[0:B, 2:3], stat[0:B, 128:384],
                                    axis=AX.X), inc=False)
        V.op(lambda e: e.tensor_scalar(stat[0:B, 128:384],
                                       stat[0:B, 128:384],
                                       smal[0:B, 2:3], None, ALU.is_ge))
        V.mark("v:oh")
        V.w("t:ohT")
        V.op(lambda e: e.tensor_copy(smal[:, 112:128], ps[4][0:128, 0:16]))
        V.mark("v:ohTcp")
        V.w("t:code")
        V.op(lambda e: e.tensor_copy(codbf[:, 0:8], ps[5][0:OM, 0:B]))
        V.mark("v:codecp")
        for osl in range(2):
            V.w(f"t:vqct:{osl}")
            V.op(lambda e, osl=osl: e.tensor_scalar(
                vqct[:, osl * 512:(osl + 1) * 512], ps[6][0:B, 0:512],
                smal[0:B, 1:2], None, ALU.mult))
            V.mark(f"v:vqctcp:{osl}")
        for osl in range(2):
            V.w(f"t:nwb:{osl}")
            V.op(lambda e, osl=osl: e.tensor_copy(
                nwb[:, osl * 512:(osl + 1) * 512], ps[7][:, :]))
            V.mark(f"v:nwbcp:{osl}")

        if PHASE < 4 or SUB4 < 1:
            return
        for i in range(2 * NTC):
            V.w(f"t:cu:{i}")
            tc, osl = i // 2, i % 2
            V.op(lambda e, i=i, tc=tc, osl=osl: e.tensor_copy(
                cu_a[:, tc * D + osl * 512:tc * D + (osl + 1) * 512],
                ps[i % 6][:, :]))
            V.mark(f"v:bfree:{i}")

        if SUB4 < 4:
            return
        seq = 2 * NTC

        def rotbf_op(t):
            V.w(f"t:rot:{t}")
            dstr = rotb if t % 2 == 0 else rotb2
            V.op(lambda e, t=t, dstr=dstr: e.tensor_scalar(
                dstr[:, :], ps[6 + t % 2][0:OM, :], smal[0:OM, 0:1], None,
                ALU.mult))
            V.mark(f"v:rotbf:{t}")
            V.w(f"y:vqr:{t}")
            V.op(lambda e: e.tensor_add(
                vqr_a[64:65, 0:D], vqr_a[64:65, 0:D], rrow(64, 0, D)))
            V.mark(f"v:vqo:{t}")

        for t in range(B):
            if t == 0:
                rotbf_op(0)
            if SUB4 == 4 and SUB4R < 1:
                continue
            for tc in range(NTC):
                if t > 0:
                    V.w(f"p:tm:{t - 1}:{tc}")
                V.w(f"q:xth:{t}:{1 if tc >= 2 else 0}")
                V.op(lambda e, t=t, tc=tc: e.tensor_scalar(
                    xt_a[:, :], xh2[:, tc * D:(tc + 1) * D],
                    stat[:, 32 + t * 4 + tc:32 + t * 4 + tc + 1],
                    stat[:, t * 4 + tc:t * 4 + tc + 1],
                    ALU.mult, ALU.add), inc=False)
                for osl in range(2):
                    i = seq
                    V.w(f"t:sp:{t}:{tc}:{osl}")
                    V.op(lambda e, i=i, tc=tc, osl=osl: e.tensor_add(
                        s_a[:, tc * D + osl * 512:tc * D + (osl + 1) * 512],
                        ps[i % 6][:, :], xt_a[:, osl * 512:(osl + 1) * 512]))
                    V.mark(f"v:bfree:{i}")
                    if osl == 1:
                        V.mark(f"v:spadd:{t}:{tc}")
                    seq += 1
            if SUB4 == 4 and SUB4R < 2:
                continue
            V.w(f"a:ms:{t}:{NTC - 1}")
            msc = stat[:, 496:500] if emit else None
            yy = (stat[:, NSC + 64:NSC + 68] if emit else None)
            r2 = smal[:, 192:196] if emit else None
            V.op(lambda e: e.tensor_scalar(msc, msc, 1.0 / D, RMS_EPS,
                                           ALU.mult, ALU.add), inc=False)
            V.op(lambda e: e.tensor_scalar(yy, msc, -0.5, 1.5, ALU.mult,
                                           ALU.add), inc=False)
            for _ in range(NEWT):
                V.op(lambda e: e.scalar_tensor_tensor(
                    r2, yy, 1.0, yy, ALU.mult, ALU.mult), inc=False)
                V.op(lambda e: e.scalar_tensor_tensor(
                    r2, r2, -0.5, msc, ALU.mult, ALU.mult), inc=False)
                V.op(lambda e: e.scalar_tensor_tensor(
                    yy, r2, 1.5, yy, ALU.add, ALU.mult), inc=False)
            V.op(lambda e: e.tensor_copy(r2, yy))
            V.mark(f"v:rstd:{t}")
            if t + 1 < B:
                rotbf_op(t + 1)
            if SUB4 < 5:
                continue
            for tc in range(NTC):
                for osl in range(2):
                    i = seq
                    V.w(f"t:h:{t}:{tc}:{osl}")
                    V.op(lambda e, i=i, tc=tc, osl=osl: e.tensor_add(
                        h_a[:, osl * 512:(osl + 1) * 512], ps[i % 6][:, :],
                        cu_a[:, tc * D + osl * 512:tc * D + (osl + 1) * 512]))
                    V.mark(f"v:bfree:{i}")
                    V.mark(f"v:hpre:{t}:{tc}:{osl}")
                    seq += 1
                V.w(f"a:hg:{t}:{tc}:1")
                V.op(lambda e: e.tensor_mul(g_a[:, :], g_a[:, :], h_a[:, :]),
                     inc=False)
                V.op(lambda e, tc=tc: e.scalar_tensor_tensor(
                    g_a[:, :], g_a[:, :], smal[:, 192 + tc:192 + tc + 1],
                    nwb[:, :], ALU.mult, ALU.mult))
                V.mark(f"v:gsc:{t}:{tc}")
        V.mark("v:done")

    def stream_pool(eng=None):
        sch.stream("pE", eng)
        P = sch

        for kk in range(NK):
            P.dma("GW", lambda e, kk=kk: e.dma_start(
                out=w3[:, kk * 3072:(kk + 1) * 3072],
                in_=io["W3T"][kk * 128:(kk + 1) * 128, :]))
            sch.mark(f"g:w3:{kk}", "GW")
        for kk in range(NK):
            P.dma("GU", lambda e, kk=kk: e.dma_start(
                out=u3[:, kk * 3072:(kk + 1) * 3072],
                in_=io["U3T"][kk * 128:(kk + 1) * 128, :]))
            sch.mark(f"g:u3:{kk}", "GU")

        for ob in range(B):
            for j in range(32):
                if j == 0:
                    P.w(f"v:negsum:{ob}")
                    if ob > 0:
                        P.w(f"v:hsevac:{ob - 1}")
                    P.op(lambda e: e.tensor_reduce(
                        stat[0:1, 489:490], stat[:, 488:489], axis=AX.C,
                        op=ALU.add))
                    P.mark(f"p:negs:{ob}")
                jg = ob * 32 + j
                if jg >= 3:
                    pj = jg - 3
                    P.w(f"t:hs:{pj // 32}:{pj % 32}")
                dst = (xhb[:, (jg % 3) * 1024:(jg % 3 + 1) * 1024]
                       if jg % 3 < 2 else bstg[:, 0:1024]) if emit else None
                P.dma(f"XB{jg % 2}", lambda e, ob=ob, j=j, dst=dst:
                      e.dma_start(
                          out=dst,
                          in_=io["XH"][ob, j * 128:(j + 1) * 128, :]))
                sch.mark(f"g:xb:{ob}:{j}", f"XB{jg % 2}")

        if PHASE < 3:
            return
        P.w("a:hssq:7:31")
        P.w("v:hsum:7:31")
        v = P.dma("GP", lambda e: e.dma_start(out=opbr[0:1, :],
                                              in_=io["OPB"][:, :]))
        P.wv("GP", v)
        sch.mark("y:opb", "GP")
        v = P.dma("GP", lambda e: e.dma_start(out=pwt[:, :],
                                              in_=io["PWT"][:, :]))
        P.wv("GP", v)
        sch.mark("y:pwt", "GP")
        P.w("v:cup:7:7")
        for kk in range(NK):
            if kk > 0:
                P.w(f"t:parchunk:{kk - 1}")
            P.dma("GP", lambda e, kk=kk: e.dma_start(
                out=opwb[:, :], in_=io["OPWT"][kk * 128:(kk + 1) * 128, :]))
            sch.mark(f"g:opw:{kk}", "GP")
        if PHASE < 4:
            return
        P.w("t:step:7")
        for c in range(NK):
            v = P.dma("GP", lambda e, c=c: e.dma_start(
                out=uo_a[:, c * D:(c + 1) * D],
                in_=io["UOTR"][:, c * D:(c + 1) * D]))
        P.wv("GP", v)
        sch.mark("g:uoa", "GP")
        for c in range(NK):
            v = P.dma("GP", lambda e, c=c: e.dma_start(
                out=wo_a[:, c * D:(c + 1) * D],
                in_=io["WOTR"][:, c * D:(c + 1) * D]))
        P.wv("GP", v)
        sch.mark("g:woa", "GP")
        v = P.dma("GP", lambda e: e.dma_start(out=oow_a[:, :],
                                              in_=io["OOWT"][:, :]))
        P.wv("GP", v)
        sch.mark("g:oowa", "GP")
        for t in range(B):
            for tc in range(NTC):
                P.w(f"v:gsc:{t}:{tc}")
                if tc >= 2:
                    P.w(f"y:st:{t}:{tc - 2}")
                elif t > 0:
                    P.w(f"y:st:{t - 1}:{tc + 2}")
                P.w(f"q:xth:{t}:{1 if tc >= 2 else 0}")
                P.op(lambda e, tc=tc: e.tensor_mul(
                    tmp_a[:, (tc % 2) * D:(tc % 2 + 1) * D], g_a[:, :],
                    s_a[:, tc * D:(tc + 1) * D]))
                P.mark(f"p:tm:{t}:{tc}")
                P.op(lambda e, tc=tc: e.tensor_add(
                    tmp_a[:, (tc % 2) * D:(tc % 2 + 1) * D],
                    tmp_a[:, (tc % 2) * D:(tc % 2 + 1) * D],
                    xh2[:, tc * D:(tc + 1) * D]))
                P.mark(f"p:out:{t}:{tc}")
        P.mark("p:done", "pE")

    def stream_sync(eng=None):
        sch.stream("qYop", eng)
        Y = sch

        def d(fn):
            return Y.dma("qY", fn)

        for fn in [
            lambda e: e.dma_start(out=rows[32:33, 0:D], in_=io["BO"][:, :]),
            lambda e: e.dma_start(out=rows[32:33, D:2 * D], in_=io["NW"][:, :]),
            lambda e: e.dma_start(out=rows[64:65, 0:D], in_=io["OOB"][:, :]),
            lambda e: e.dma_start(out=rows[64:65, D:2 * D], in_=io["PB"][:, :]),
            lambda e: e.dma_start(out=rows[0:1, 512:512 + OM],
                                  in_=io["VQB"][:, :]),
            lambda e: e.dma_start(out=rows[0:1, 1024:2048],
                                  in_=io["INC"][:, :]),
            lambda e: e.dma_start(out=smal[0:1, 0:2], in_=io["MIX"][:, :]),
            lambda e: e.dma_start(out=idt[:, :], in_=io["IDT"][:, :]),
            lambda e: e.dma_start(
                out=cb2[:, :],
                in_=io["CB"].rearrange("(c p) e -> p c e", p=128)),
            lambda e: e.dma_start(out=xh2[0:1, 0:3 * D], in_=io["B3"][:, :]),
        ]:
            Y.dma("Y0", fn)
        for lbl in ["y:rows", "y:mix", "y:idt", "y:cb", "y:b3"]:
            Y.mark(lbl, "Y0")

        def xnbt(t):
            Y.w(f"y:scwb0:{t}")
            Y.w(f"y:scwb1:{t}")
            if t >= 2:
                Y.w(f"t:step:{t - 2}")
            for kk in range(NK):
                Y.dma(f"N{t % 2}", lambda e, t=t, kk=kk: e.dma_start_transpose(
                    xnb[:, (t % 2) * 4096 + kk * 512:
                        (t % 2) * 4096 + (kk + 1) * 512],
                    io["xn_sc"][t, :, kk * 128:(kk + 1) * 128]))
            Y.mark(f"y:xnb:{t}", f"N{t % 2}")

        def shd(ob):
            Y.w(f"v:hsevac:{ob}")
            Y.dma("H0", lambda e, ob=ob: e.dma_start(
                out=io["hm_sc"][ob:ob + 1, :], in_=xnh[0:1, 0:D]))
            Y.mark(f"y:shd:{ob}", "H0")

        def p1load(u):
            kindu, ob, j = u
            ts = tile_seq[(kindu, ob, j)]
            if ts < 3:
                Y.w("t:b3t")
            if ts >= 4:
                Y.w(tile_done_label(tile_list[ts - 4]))
            srcio = (io["XH"] if kindu == "p1t" else io["XS"]) if emit \
                else None
            Y.dma(f"XL{ts % 4}", lambda e, ob=ob, j=j, ts=ts, srcio=srcio:
                  e.dma_start(
                      out=xh2[:, (ts % 4) * D:(ts % 4 + 1) * D],
                      in_=srcio[ob, j * 128:(j + 1) * 128, :]))
            lbl = f"q:x1:{ob}:{j}" if kindu == "p1t" else f"q:xo:{ob}:{j}"
            sch.mark(lbl, f"XL{ts % 4}")

        _order = [(_t, _k) for _t in range(B) for _k in range(NK)]
        due = {}
        for _i, _tk in enumerate(_order):
            kind = slot_plan[_tk]
            if kind[0] == "own" and kind[1] >= 1:
                _b = kind[1]
                _j = max(_i + 1, _order.index((_b - 1, 0)))
                _j = min(_j, len(_order) - 1)
                due.setdefault(_order[_j], []).append(("xnbt", _b))
            elif kind[0] == "units":
                for _u in kind[1]:
                    if _u[0] == "p2t" and _u[2] == 31:
                        _j = min(_i + 1, len(_order) - 1)
                        due.setdefault(_order[_j], []).append(("shd", _u[1]))
        placed = set()
        xnbt(0)
        for t in range(B):
            for k in range(NK):
                kind = slot_plan[(t, k)]
                if kind[0] == "units":
                    for u in kind[1]:
                        if u[0] in ("p1t", "o1t"):
                            p1load(u)
                for what, idx in due.get((t, k), []):
                    if what == "xnbt":
                        xnbt(idx)
                    else:
                        shd(idx)
                    placed.add((what, idx))
        for u in hm_tail:
            if u[0] in ("p1t", "o1t"):
                p1load(u)
        for ob in range(B):
            if ("shd", ob) not in placed:
                shd(ob)

        if PHASE < 2 or (PHASE == 2 and SUB2 < 1):
            return
        for bidx in range(2 * HB):
            if bidx == 0:
                Y.w(f"y:shd:{B - 1}")
            elif not (PHASE == 2 and SUB2 < 2):
                Y.w(f"t:hmT:{bidx - 1}")
            Y.dma("RB", lambda e, bidx=bidx: e.dma_start(
                out=xnh[0:1, 0:D], in_=io["hm_sc"][bidx:bidx + 1, :]))
            Y.mark(f"y:hmrow:{bidx}", "RB")

        if PHASE < 3:
            return
        Y.w("a:hssq:7:31")
        Y.w("v:hsum:7:31")
        Y.dma("VW", lambda e: e.dma_start(
            out=vqw[:, :],
            in_=io["VQWT"].rearrange("(k p) e -> p k e", p=128)))
        Y.mark("y:vqw", "VW")
        Y.w("v:pst:3")
        Y.w("v:omz")
        for r in range(63):
            ln = 63 - r
            off = _triu_off(r)
            Y.dma(f"XL{r % 4}", lambda e, r=r, ln=ln, off=off: e.dma_start(
                out=omu[r:r + 1, :, r + 1:64],
                in_=par_sb[0:B, off:off + ln],
                allow_slow_non_contiguous=(ln <= 2)))
        for _q in ["XL0", "XL1", "XL2", "XL3"]:
            sch.mark(f"g:scat:{_q}", _q)


        if PHASE < 4 or SUB4 < 2:
            return
        Y.w("v:vqctcp:1")
        Y.dma("VQ", lambda e: e.dma_start(out=io["vq_sc"][:, :],
                                          in_=vqct[:, :]))
        Y.mark("y:vqwr", "VQ")
        for t in range(B):
            Y.w("t:step:7")
            if t >= 2:
                if SUB4 >= 5:
                    Y.w(f"t:h:{t - 2}:{NTC - 1}:1")
                elif SUB4 == 4 and SUB4R >= 1:
                    Y.w(f"t:sp:{t - 2}:{NTC - 1}:1")
            for kk in range(NK):
                Y.dma(f"N{t % 2}", lambda e, t=t, kk=kk: e.dma_start_transpose(
                    xnb[:, (t % 2) * 4096 + kk * 512:
                        (t % 2) * 4096 + (kk + 1) * 512],
                    io["xn_sc"][t, :, kk * 128:(kk + 1) * 128]))
            Y.mark(f"y:xnbo:{t}", f"N{t % 2}")
            if t == 0:
                Y.w("y:vqwr")
            elif SUB4 >= 5 or (SUB4 == 4 and SUB4R >= 1):
                Y.w(f"t:sp:{t - 1}:{NTC - 1}:1")
            Y.dma("VQ", lambda e, t=t: e.dma_start(
                out=vqr_a[64:65, :], in_=io["vq_sc"][t:t + 1, :]))
            Y.mark(f"y:vqr:{t}", "VQ")
            if t > 0 and SUB4 >= 5:
                for tc in range(NTC):
                    Y.w(f"p:out:{t - 1}:{tc}")
                    Y.dma(f"ST{tc % 2}", lambda e, t=t, tc=tc: e.dma_start(
                        out=io["OUT"][t - 1, tc * 128:(tc + 1) * 128, :],
                        in_=tmp_a[:, (tc % 2) * D:(tc % 2 + 1) * D]))
                    Y.mark(f"y:st:{t - 1}:{tc}", f"ST{tc % 2}")
        if SUB4 < 5:
            return
        for tc in range(NTC):
            Y.w(f"p:out:{B - 1}:{tc}")
            Y.dma(f"ST{tc % 2}", lambda e, tc=tc: e.dma_start(
                out=io["OUT"][B - 1, tc * 128:(tc + 1) * 128, :],
                in_=tmp_a[:, (tc % 2) * D:(tc % 2 + 1) * D]))
            Y.mark(f"y:st:{B - 1}:{tc}", f"ST{tc % 2}")
        if DEBUG:
            Y.w("v:done")
            Y.w("a:done")
            for fn in [
                lambda e: e.dma_start(out=io["DBG_HM"][:, :],
                                      in_=smal[:, 128:192]),
                lambda e: e.dma_start(out=io["DBG_G"][:, :], in_=gmat[:, :]),
                lambda e: e.dma_start(out=io["DBG_VQ"][:, :], in_=vqct[:, :]),
                lambda e: e.dma_start(out=io["DBG_CF"][:, :],
                                      in_=cfp[:, 0:NK * TOK]),
                lambda e: e.dma_start(out=io["DBG_XNB"][:, :],
                                      in_=xnb[:, 0:NK * TOK]),
                lambda e: e.dma_start(out=io["DBG_S"][:, :], in_=s_a[:, :]),
            ]:
                v = Y.dma("RB", fn)
                Y.wv("RB", v)
        if emit:
            Y.cur.wait_ge(sch.sems["ST0"], sch.cnt.get("ST0", 0))
            Y.cur.wait_ge(sch.sems["ST1"], sch.cnt.get("ST1", 0))

    if emit:
        @block.tensor
        def _(eng):
            stream_tensor(eng)

        @block.scalar
        def _(eng):
            stream_scalar(eng)

        @block.vector
        def _(eng):
            stream_vector(eng)

        @block.gpsimd
        def _(eng):
            stream_pool(eng)

        @block.sync
        def _(eng):
            stream_sync(eng)
    else:
        stream_tensor()
        stream_scalar()
        stream_vector()
        stream_pool()
        stream_sync()
    ctx.close()


_NC_CACHE = [None]


def kernel(**inputs):
    x = np.ascontiguousarray(np.asarray(inputs["L0_attn_x"], np.float32))

    def T(a):
        return np.ascontiguousarray(np.asarray(a, np.float32).T)

    def R(a, shape):
        return np.ascontiguousarray(np.asarray(a, np.float32).reshape(shape))

    def KR(a):
        return np.ascontiguousarray(
            np.asarray(a, np.float32).T.reshape(NK, 128, D)
            .transpose(1, 0, 2).reshape(128, NK * D))

    w3t = np.ascontiguousarray(np.concatenate(
        [T(inputs["L0_ffn_wf"]), T(inputs["L0_ffn_wi"]),
         T(inputs["L0_ffn_wc"])], axis=1))
    u3t = np.ascontiguousarray(np.concatenate(
        [T(inputs["L0_ffn_uf"]), T(inputs["L0_ffn_ui"]),
         T(inputs["L0_ffn_uc"])], axis=1))
    b3 = np.ascontiguousarray(np.concatenate(
        [R(inputs["L0_ffn_bf"], (1, D)), R(inputs["L0_ffn_bi"], (1, D)),
         R(inputs["L0_ffn_bc"], (1, D))], axis=1))
    mix = np.array([[np.float32(inputs["L0_attn_omega_mix"]),
                     np.float32(inputs["L0_attn_vq_mix"])]], np.float32)
    common = {
        "W3T": w3t, "U3T": u3t,
        "WOTR": KR(inputs["L0_ffn_wo"]), "UOTR": KR(inputs["L0_ffn_uo"]),
        "B3": b3, "BO": R(inputs["L0_ffn_bo"], (1, D)),
        "INC": R(inputs["L0_ffn_in_c"], (1, D)),
        "OPWT": T(inputs["L0_attn_omega_proj_w"]),
        "OPB": R(inputs["L0_attn_omega_proj_b"], (1, NPAR)),
        "OOWT": T(inputs["L0_attn_omega_out_w"]),
        "OOB": R(inputs["L0_attn_omega_out_b"], (1, D)),
        "VQWT": T(inputs["L0_attn_vq_proj_in_w"]),
        "VQB": R(inputs["L0_attn_vq_proj_in_b"], (1, OM)),
        "PWT": T(inputs["L0_attn_vq_proj_out_w"]),
        "PB": R(inputs["L0_attn_vq_proj_out_b"], (1, D)),
        "CB": np.ascontiguousarray(
            np.asarray(inputs["L0_attn_codebook"], np.float32)),
        "NW": R(inputs["L0_attn_norm_w"], (1, D)),
        "MIX": mix, "XH": x,
        "IDT": np.eye(128, dtype=np.float32),
    }
    in_maps = []
    for c in range(NCORE):
        m = dict(common)
        m["XS"] = np.ascontiguousarray(x[:, c * TOK:(c + 1) * TOK, :])
        in_maps.append(m)

    if _NC_CACHE[0] is None:
        _NC_CACHE[0] = build_program()
    nc = _NC_CACHE[0]
    res = run_bass_kernel_spmd(nc, in_maps, core_ids=list(range(NCORE)))
    out = np.concatenate(
        [np.asarray(res.results[c]["OUT"]) for c in range(NCORE)], axis=1)
    return out


if __name__ == "__main__":
    print("building program...")
    prog = build_program()
    try:
        print("built; instructions:", prog.n_instructions())
    except Exception:
        print("built")


# version 26
# speedup vs baseline: 1.1847x; 1.1847x over previous
import contextlib

import numpy as np
import ml_dtypes
import concourse.bass as bass
import concourse.mybir as mybir
from concourse.bass_utils import run_bass_kernel_spmd

F32 = mybir.dt.float32
BF16 = mybir.dt.bfloat16
I32 = mybir.dt.int32
AF = mybir.ActivationFunctionType
ALU = mybir.AluOpType
AX = mybir.AxisListType
ds = bass.ds

D = 1024
B = 8
L = 4096
NCORE = 8
TOK = 512
NK = 8
NTC = 4
HB = 4
OM = 64
NCODE = 256
NPAR = OM * (OM - 1) // 2
LN_EPS = 1e-5
RMS_EPS = 1e-6
NEWT = 4

DEBUG = False
PHASE = 4
SUB2 = 2
SUB4 = 5
SUB4R = 2
SUB4S = 3
PART_BYTES = 207 * 1024


def _triu_off(r):
    return r * 63 - (r * (r - 1)) // 2


class Sched:
    CURNC = None

    """Two-pass emitter: pass 1 counts semaphore values and records labels,
    pass 2 emits instructions resolving waits from the recorded labels."""

    def __init__(self, emit, labels):
        self.emit = emit
        self.cnt = {}
        self.labels = labels
        self.sems = {}
        self.cur = None
        self.curname = None

    def stream(self, name, eng):
        self.curname = name
        self.cur = eng

    def inc(self, sem, n=1):
        self.cnt[sem] = self.cnt.get(sem, 0) + n
        return self.cnt[sem]

    def mark(self, label, sem=None):
        sem = sem or self.curname
        self.labels[label] = (sem, self.cnt.get(sem, 0))

    def w(self, label):
        if self.emit:
            sem, val = self.labels[label]
            if val > 0:
                self.cur.wait_ge(self.sems[sem], val)

    def wv(self, sem, val):
        if self.emit and val > 0:
            self.cur.wait_ge(self.sems[sem], val)

    SERIAL = ("aE", "vE", "pE")

    TAGLOG = {}

    def _tag(self, info):
        if self.emit and Sched.CURNC is not None:
            Sched.TAGLOG[Sched.CURNC.n_instructions()] = info

    def op(self, fn, inc=True):
        if self.curname in self.SERIAL:
            self.wv(self.curname, self.cnt.get(self.curname, 0))
            v = self.inc(self.curname, 1)
            if self.emit:
                fn(self.cur).then_inc(self.sems[self.curname], 1)
                self._tag((self.curname, v))
            return v
        v = self.inc(self.curname, 1) if inc else None
        if self.emit:
            ins = fn(self.cur)
            if inc:
                ins.then_inc(self.sems[self.curname], 1)
            self._tag((self.curname, v))
        return v

    def dma(self, qsem, fn):
        if self.curname in self.SERIAL:
            self.wv(self.curname, self.cnt.get(self.curname, 0))
        v = self.inc(qsem, 16)
        if self.emit:
            fn(self.cur).then_inc(self.sems[qsem], 16)
            self._tag((qsem, v))
        return v


def build_program(detect_races=True):
    labels = {}
    s1 = Sched(False, labels)
    _emit(s1, None)
    nc = bass.Bass("TRN2", target_bir_lowering=False,
                   detect_race_conditions=detect_races)
    Sched.CURNC = nc
    s2 = Sched(True, dict(labels))
    _emit(s2, nc)
    Sched.CURNC = None
    for k in set(s1.cnt) | set(s2.cnt):
        assert s1.cnt.get(k, 0) == s2.cnt.get(k, 0), (
            "pass count mismatch", k, s1.cnt.get(k, 0), s2.cnt.get(k, 0))
    lab2 = {}
    s3 = Sched(False, lab2)
    _emit(s3, None)
    for k, v in lab2.items():
        assert labels[k] == v, ("label drift", k, labels[k], v)
    return nc


def _emit(sch, nc):
    emit = sch.emit
    ctx = contextlib.ExitStack()

    io = {}
    if emit:
        dt = nc.dram_tensor
        io["XS"] = dt("XS", [B, TOK, D], F32, kind="ExternalInput")
        io["XH"] = dt("XH", [B, L, D], F32, kind="ExternalInput")
        io["W3T"] = dt("W3T", [D, 3 * D], F32, kind="ExternalInput")
        io["U3T"] = dt("U3T", [D, 3 * D], F32, kind="ExternalInput")
        io["WOTR"] = dt("WOTR", [128, NK * D], F32, kind="ExternalInput")
        io["UOTR"] = dt("UOTR", [128, NK * D], F32, kind="ExternalInput")
        io["B3"] = dt("B3", [1, 3 * D], F32, kind="ExternalInput")
        io["BO"] = dt("BO", [1, D], F32, kind="ExternalInput")
        io["INC"] = dt("INC", [1, D], F32, kind="ExternalInput")
        io["OPWT"] = dt("OPWT", [D, NPAR], F32, kind="ExternalInput")
        io["OPB"] = dt("OPB", [1, NPAR], F32, kind="ExternalInput")
        io["OOWT"] = dt("OOWT", [OM, D], F32, kind="ExternalInput")
        io["OOB"] = dt("OOB", [1, D], F32, kind="ExternalInput")
        io["VQWT"] = dt("VQWT", [D, OM], F32, kind="ExternalInput")
        io["VQB"] = dt("VQB", [1, OM], F32, kind="ExternalInput")
        io["PWT"] = dt("PWT", [OM, D], F32, kind="ExternalInput")
        io["PB"] = dt("PB", [1, D], F32, kind="ExternalInput")
        io["CB"] = dt("CB", [NCODE, OM], F32, kind="ExternalInput")
        io["NW"] = dt("NW", [1, D], F32, kind="ExternalInput")
        io["MIX"] = dt("MIX", [1, 2], F32, kind="ExternalInput")
        io["IDT"] = dt("IDT", [128, 128], F32, kind="ExternalInput")
        io["OUT"] = dt("OUT", [B, TOK, D], F32, kind="ExternalOutput")
        io["xn_sc"] = dt("xn_sc", [B, TOK, D], BF16, kind="Internal")
        io["hm_sc"] = dt("hm_sc", [B, D], F32, kind="Internal")
        io["vq_sc"] = dt("vq_sc", [B, D], F32, kind="Internal")
        if DEBUG:
            io["DBG_XNB"] = dt("DBG_XNB", [128, NK * TOK], BF16,
                               kind="ExternalOutput")
            io["DBG_CF"] = dt("DBG_CF", [128, NK * TOK], F32,
                              kind="ExternalOutput")
            io["DBG_HM"] = dt("DBG_HM", [128, 64], F32, kind="ExternalOutput")
            io["DBG_G"] = dt("DBG_G", [OM, B * OM], F32, kind="ExternalOutput")
            io["DBG_VQ"] = dt("DBG_VQ", [B, D], F32, kind="ExternalOutput")
            io["DBG_S"] = dt("DBG_S", [128, NTC * D], F32,
                             kind="ExternalOutput")

    offs = {}
    base_addr = [0]
    if emit:
        guard = ctx.enter_context(
            nc.sbuf_tensor("arena", [128, PART_BYTES], mybir.dt.uint8))
        base_addr[0] = nc.lookup_mloc(guard).addr

    def carve(name, off, shape, dtype):
        per = int(np.prod(shape[1:])) * mybir.dt.size(dtype)
        assert off % 32 == 0 and off + per <= PART_BYTES, (name, off, per)
        offs[name] = (off, per)
        if not emit:
            return None
        h = nc.virtual_sbuf_tensor(name, list(shape), dtype)
        nc._tensor(h.name, list(h.shape), h.dtype, type="SB")
        ml = nc.lookup_mloc(h)
        ml.allocated = True
        ml.addr = base_addr[0] + off
        return h

    o = [0]

    def C(name, shape, dtype):
        per = int(np.prod(shape[1:])) * mybir.dt.size(dtype)
        h = carve(name, o[0], shape, dtype)
        o[0] += (per + 31) & ~31
        return h

    w3 = C("w3", [128, NK * 3 * D], BF16)
    u3 = C("u3", [128, NK * 3 * D], BF16)
    cfp = C("cfp", [128, NK * TOK], F32)
    cbf = C("cbf", [128, 2 * NK * TOK], BF16)
    xnb = C("xnb", [128, 2 * NK * TOK], BF16)
    gst = C("gst", [128, 3 * TOK], F32)
    xh2 = C("xh2", [128, 4 * D], F32)
    xs1 = C("xs1", [128, D], F32)
    xnh = C("xnh", [128, D], F32)
    sacc = C("sacc", [128, D], F32)
    ftmp = C("ftmp", [64, D], F32)
    rows = C("rows", [128, 2048], F32)
    xtk = C("xtk", [128, 2 * D], BF16)
    bstg = C("bstg", [128, D], BF16)
    vqct = C("vqct", [B, D], F32)
    gbf = C("gbf", [OM, B * OM], BF16)
    rotb = C("rotb", [OM, TOK], BF16)
    cb2 = C("cb2", [128, 2 * OM], F32)
    stat = C("stat", [128, 512], F32)
    smal = C("smal", [128, 256], F32)
    idt = C("idt", [128, 128], F32)
    onesml = C("onesml", [128, 128], F32)
    onesbf = C("onesbf", [1, 16], BF16)
    assert o[0] <= PART_BYTES, o[0]

    aoff = [offs["w3"][0]]

    def A(name, shape, dtype):
        per = int(np.prod(shape[1:])) * mybir.dt.size(dtype)
        h = carve(name, aoff[0], shape, dtype)
        aoff[0] += (per + 31) & ~31
        return h

    rotb2 = carve("rotb2", offs["gst"][0] + 4032, [OM, TOK], BF16)
    par_sb = carve("par_sb", offs["cfp"][0], [B, 4 * 504], F32)
    wo_a = A("wo_a", [128, NK * D], BF16)
    uo_a = A("uo_a", [128, NK * D], BF16)
    oow_a = A("oow_a", [OM, D], BF16)
    cu_a = A("cu_a", [128, NTC * D], F32)
    s_a = A("s_a", [128, NTC * D], F32)
    xt_a = A("xt_a", [128, D], F32)
    h_a = A("h_a", [128, D], F32)
    g_a = A("g_a", [128, D], F32)
    tmp_a = A("tmp_a", [128, 2 * D], F32)
    vqr_a = A("vqr_a", [128, D], F32)
    nwb = A("nwb", [128, D], F32)
    assert aoff[0] <= offs["u3"][0] + offs["u3"][1]

    xo = offs["xh2"][0]
    omu = carve("omu", xo, [OM, B, OM], F32)
    om2 = carve("om2", xo + 2048, [OM, B * OM], F32)
    opbr = carve("opbr", xo + 4096, [1, NPAR], BF16)
    pwt = carve("pwt", xo + 8160, [OM, D], BF16)
    vqw = carve("vqw", xo + 10208, [128, NK * OM], F32)
    gmat = carve("gmat", xo + 12256, [OM, B * OM], F32)
    cbn = carve("cbn", xo + 14304, [128, 2 * OM], F32)
    cbnT = carve("cbnT", xo + 14816, [OM, NCODE], F32)
    hmbf = carve("hmbf", xo + 15840, [128, 64], BF16)
    codbf = carve("codbf", xo + 15968, [OM, 32], BF16)
    opwb = carve("opwb", offs["gst"][0], [128, NPAR], BF16)
    xhb = carve("xhb", offs["sacc"][0], [128, 2048], BF16)
    rbf = carve("rbf", offs["stat"][0] + 64 * 4, [128, 64], BF16)
    xs1b = carve("xs1b", offs["ftmp"][0], [128, D], F32)

    NSC = 392

    if emit:
        for s in ["Y0", "W0", "W1", "N0", "N1", "S1", "S1B", "X0", "X1",
                  "XL0", "XL1", "XL2", "XL3", "XB0", "XB1",
                  "H0", "RB", "PW", "VW", "VQ", "ST0", "ST1", "GW", "GU",
                  "GB", "GP", "tE", "aE", "vE", "pE"]:
            sch.sems[s] = ctx.enter_context(nc.semaphore(s))
        block = ctx.enter_context(nc.Block())
        ps = [ctx.enter_context(nc.psum_tensor(f"psum{i}", [128, TOK], F32))
              for i in range(8)]
    else:
        ps = [None] * 8

    def rrow(level, c0, c1):
        return rows[level:level + 1, c0:c1]

    def tidx(t, k, g):
        return t * 24 + k * 3 + g

    slot_plan = {}
    _s1 = []
    for _ob in range(B):
        _s1 += [("p1t", _ob, _j) for _j in range(32)]
        _s1.append(("newt", _ob))
    hm_units = list(_s1[:41])
    _rest = _s1[41:]
    for _ob in range(B):
        _blk2 = [("p2t", _ob, _j) for _j in range(32)]
        _take1 = _rest[:33]
        _rest = _rest[33:]
        for _i in range(max(len(_blk2), len(_take1))):
            if _i < len(_blk2):
                hm_units.append(_blk2[_i])
            if _i < len(_take1):
                hm_units.append(_take1[_i])
    assert len(hm_units) == 520 and not _rest, (len(hm_units), len(_rest))
    _nxt0 = [1]
    _slots0 = []
    for _t in range(B):
        for _k in range(NK):
            if _nxt0[0] < B and _k % 3 == 2:
                _slots0.append(("own", _nxt0[0]))
                _nxt0[0] += 1
            else:
                _slots0.append("u")
    _own_upos = {}
    _uc = 0
    for _s in _slots0:
        if _s == "u":
            _uc += 10
        else:
            _own_upos[_s[1]] = _uc
    for _b in sorted(_own_upos, reverse=True):
        _pos = max(0, _own_upos[_b] - 15 - 5 * (_b - 1))
        ins = [("o1t", _b, _i) for _i in range(4)] + [("onewt", _b)]
        hm_units[_pos:_pos] = ins
    tile_seq = {}
    tile_list = []
    for _u in hm_units:
        if _u[0] in ("p1t", "o1t"):
            tile_seq[(_u[0], _u[1], _u[2])] = len(tile_list)
            tile_list.append(_u)
    _ui = [0]
    _nxt = [1]
    for _t in range(B):
        for _k in range(NK):
            if _nxt[0] < B and _k % 3 == 2:
                slot_plan[(_t, _k)] = ("own", _nxt[0])
                _nxt[0] += 1
            else:
                take = hm_units[_ui[0]:_ui[0] + 10]
                _ui[0] += len(take)
                slot_plan[(_t, _k)] = ("units", take)
    hm_tail = hm_units[_ui[0]:]

    def tile_done_label(u):
        if u[0] == "p1t":
            return f"a:hssq:{u[1]}:{u[2]}"
        return f"a:ssqo:{u[1]}:{u[2]}"

    def stream_tensor(eng=None):
        sch.stream("tE", eng)
        T = sch

        T.w("y:rows")
        T.w("v:ones")
        for k in range(NK):
            if k >= 2:
                T.w(f"v:c0cp:{k - 2}")
            T.op(lambda e, k=k: e.matmul(
                ps[6 + k % 2][:, :],
                rrow(0, 1024 + k * 128, 1024 + (k + 1) * 128),
                rrow(0, 0, 512), start=True, stop=True))
            T.mark(f"t:c0:{k}")
        T.w("y:b3")
        T.w("y:idt")
        T.w("v:c0cp:7")
        for j in range(24):
            T.op(lambda e, j=j: e.transpose(
                ps[7][0:128, j:j + 1], xh2[0:1, j * 128:(j + 1) * 128],
                idt[0:1, 0:1]), inc=(j == 23))
        T.mark("t:b3t")

        def hs_mm(ob, j):
            if j == 0:
                if ob == 0:
                    T.w("v:c0cp:7")
                    T.w("v:b3c")
                else:
                    T.w(f"v:hsevac:{ob - 1}")
                T.w(f"v:rbf:{ob}")
            T.w(f"g:xb:{ob}:{j}")
            rcol = (ob % 2) * 32 + j
            sl3 = (ob * 32 + j) % 3
            rh = (xhb[:, sl3 * 1024:(sl3 + 1) * 1024] if sl3 < 2
                  else bstg[:, 0:1024]) if emit else None
            T.op(lambda e, rcol=rcol, rh=rh, j=j: e.matmul(
                ps[6][0:1, 0:512], rbf[:, rcol:rcol + 1], rh[:, 0:512],
                start=(j == 0), stop=(j == 31), skip_group_check=True),
                inc=False)
            T.op(lambda e, rcol=rcol, rh=rh, j=j: e.matmul(
                ps[7][0:1, 0:512], rbf[:, rcol:rcol + 1], rh[:, 512:1024],
                start=(j == 0), stop=(j == 31), skip_group_check=True))
            T.mark(f"t:hs:{ob}:{j}")

        for t in range(B):
            par = t % 2
            for k in range(NK):
                for g in range(3):
                    ti = tidx(t, k, g)
                    bank = ti % 6
                    col = g * D + k * 128
                    if ti >= 6:
                        pt = (ti - 6) // 24
                        pk = ((ti - 6) % 24) // 3
                        pg = (ti - 6) % 3
                        if pg == 0:
                            T.w(f"v:cup:{pt}:{pk}")
                        else:
                            T.w(f"a:ev:{pt}:{pk}:{pg}")
                    T.w(f"y:xnb:{t}")
                    for kk in range(NK):
                        T.w(f"g:w3:{kk}")
                        T.op(lambda e, bank=bank, kk=kk, col=col, t=t:
                             e.matmul(
                                 ps[bank][:, :],
                                 w3[:, kk * 3072 + col:kk * 3072 + col + 128],
                                 xnb[:, (t % 2) * 4096 + kk * 512:
                                     (t % 2) * 4096 + (kk + 1) * 512],
                                 start=(kk == 0), stop=False), inc=False)
                    for kk in range(NK):
                        T.w(f"g:u3:{kk}")
                        if t == 0:
                            T.w(f"v:c0cp:{kk}")
                        else:
                            T.w(f"v:cup:{t - 1}:{kk}")
                        T.op(lambda e, bank=bank, kk=kk, col=col, par=par:
                             e.matmul(
                                 ps[bank][:, :],
                                 u3[:, kk * 3072 + col:kk * 3072 + col + 128],
                                 cbf[:, par * 4096 + kk * 512:
                                     par * 4096 + (kk + 1) * 512],
                                 start=False, stop=(kk == NK - 1)),
                             inc=(kk == NK - 1))
                    T.mark(f"t:mm:{t}:{k}:{g}")
                kind = slot_plan[(t, k)]
                if kind[0] == "units":
                    for u in kind[1]:
                        if u[0] == "p2t":
                            hs_mm(u[1], u[2])
            T.mark(f"t:step:{t}")

        for u in hm_tail:
            if u[0] == "p2t":
                hs_mm(u[1], u[2])
        if PHASE < 2 or (PHASE == 2 and SUB2 < 2):
            return
        T.w("a:ev:7:7:2")
        T.w("v:cup:7:7")
        for bidx in range(2 * HB):
            T.w(f"y:hmrow:{bidx}")
            if bidx > 0:
                T.w(f"v:hmb:{bidx - 1}")
            for kk in range(NK):
                T.op(lambda e, kk=kk: e.transpose(
                    ps[1][0:128, kk:kk + 1], xnh[0:1, kk * 128:(kk + 1) * 128],
                    idt[0:1, 0:1]), inc=(kk == NK - 1))
            T.mark(f"t:hmT:{bidx}")
        if PHASE < 3:
            return
        T.w("y:mix")
        T.w("v:hsevac:7")
        T.op(lambda e: e.matmul(ps[7][:, 0:2], rrow(0, 0, 128),
                                smal[0:1, 0:2], start=True, stop=True))
        T.mark("t:mixb")
        T.w("v:mixcp")
        for kk in range(NK):
            T.w(f"g:opw:{kk}")
            T.w("v:hmbf")
            for q in range(4):
                T.op(lambda e, kk=kk, q=q: e.matmul(
                    ps[2 + q][0:B, 0:504], hmbf[:, kk * 8:(kk + 1) * 8],
                    opwb[:, q * 504:(q + 1) * 504],
                    start=(kk == 0), stop=False, skip_group_check=True),
                    inc=(q == 3))
            T.mark(f"t:parchunk:{kk}")
        T.w("y:opb")
        for q in range(4):
            T.op(lambda e, q=q: e.matmul(
                ps[2 + q][0:B, 0:504], onesbf[0:1, 0:8],
                opbr[0:1, q * 504:(q + 1) * 504], start=False, stop=True,
                skip_group_check=True))
            T.mark(f"t:par:{q}")
        for b in range(B):
            if b == 0:
                for _q in ["XL0", "XL1", "XL2", "XL3"]:
                    T.w(f"g:scat:{_q}")
            T.w("v:mixcp" if b == 0 else f"v:gbf:{b - 1}")
            T.op(lambda e, b=b: e.transpose(
                ps[7][0:OM, 0:OM], omu[0:OM, b, :], idt[0:OM, 0:OM]))
            T.mark(f"t:omT:{b}")
            T.w(f"v:omfull:{b}")
            T.op(lambda e, b=b: e.matmul(
                ps[6][0:OM, 0:OM], om2[0:OM, b * OM:(b + 1) * OM],
                omu[0:OM, b, :], start=True, stop=True))
            T.mark(f"t:om2:{b}")
            T.w(f"v:om2cp:{b}")
            T.op(lambda e, b=b: e.matmul(
                ps[7][0:OM, 0:OM], om2[0:OM, b * OM:(b + 1) * OM],
                omu[0:OM, b, :], start=True, stop=True))
            T.mark(f"t:om3:{b}")
        T.w("v:hm")
        T.w("y:vqw")
        for kk in range(NK):
            T.op(lambda e, kk=kk: e.matmul(
                ps[0][0:B, 0:OM], smal[:, 128 + kk * 8:128 + (kk + 1) * 8],
                vqw[:, kk * OM:(kk + 1) * OM],
                start=(kk == 0), stop=False), inc=False)
        T.op(lambda e: e.matmul(ps[0][0:B, 0:OM], rrow(0, 0, 8),
                                rrow(0, 512, 512 + OM),
                                start=False, stop=True))
        T.mark("t:vqin")
        T.w("v:cbn")
        T.w("v:hmb:7")
        for j in range(2):
            if j > 0:
                T.w("v:cbnTcp:0")
            T.op(lambda e, j=j: e.transpose(
                ps[1][0:OM, 0:128], cbn[:, j * OM:(j + 1) * OM],
                idt[0:128, 0:128]))
            T.mark(f"t:cbnT:{j}")
        T.w("v:vqi")
        T.w("v:pst:0")
        T.op(lambda e: e.transpose(ps[2][0:OM, 0:B], smal[0:B, 32:96],
                                   idt[0:B, 0:B]))
        T.mark("t:vqiT")
        T.w("v:vqiTcp")
        T.w("v:cbnTcp:1")
        T.w("v:pst:1")
        T.op(lambda e: e.matmul(ps[3][0:B, 0:NCODE], smal[0:OM, 104:112],
                                cbnT[:, :], start=True, stop=True))
        T.mark("t:scores")
        T.w("v:oh")
        T.w("v:pst:2")
        for j in range(2):
            T.op(lambda e, j=j: e.transpose(
                ps[4][0:128, j * 8:(j + 1) * 8],
                stat[0:B, 128 + j * 128:128 + (j + 1) * 128],
                idt[0:B, 0:B]), inc=(j == 1))
        T.mark("t:ohT")
        T.w("v:ohTcp")
        T.w("v:pst:3")
        for j in range(2):
            T.op(lambda e, j=j: e.matmul(
                ps[5][0:OM, 0:B], cb2[:, j * OM:(j + 1) * OM],
                smal[:, 112 + j * 8:112 + (j + 1) * 8],
                start=(j == 0), stop=(j == 1)), inc=(j == 1))
        T.mark("t:code")
        T.w("v:codecp")
        T.w("y:pwt")
        for osl in range(2):
            T.op(lambda e, osl=osl: e.matmul(
                ps[6][0:B, 0:512], codbf[:, 0:8],
                pwt[:, osl * 512:(osl + 1) * 512],
                start=True, stop=False), inc=False)
            T.op(lambda e, osl=osl: e.matmul(
                ps[6][0:B, 0:512], onesml[64:65, 0:8],
                rrow(64, D + osl * 512, D + (osl + 1) * 512),
                start=False, stop=True))
            T.mark(f"t:vqct:{osl}")
            T.w(f"v:vqctcp:{osl}")
        T.w("v:gbf:7")
        for osl in range(2):
            T.op(lambda e, osl=osl: e.matmul(
                ps[7][:, :], onesml[32:33, 0:128],
                rrow(32, D + osl * 512, D + (osl + 1) * 512),
                start=True, stop=True))
            T.mark(f"t:nwb:{osl}")
            T.w(f"v:nwbcp:{osl}")

        if PHASE < 4 or SUB4 < 1:
            return
        seq = [0]

        def bankwait(i):
            if i >= 6:
                T.w(f"v:bfree:{i - 6}")

        fpar = B % 2
        for tc in range(NTC):
            for osl in range(2):
                i = seq[0]
                bank = i % 6
                bankwait(i)
                T.w("g:uoa")
                for kk in range(NK):
                    T.op(lambda e, bank=bank, kk=kk, tc=tc, osl=osl:
                         e.matmul(
                             ps[bank][:, :],
                             cbf[:, fpar * 4096 + kk * 512 + tc * 128:
                                 fpar * 4096 + kk * 512 + tc * 128 + 128],
                             uo_a[:, kk * D + osl * 512:
                                  kk * D + (osl + 1) * 512],
                             start=(kk == 0), stop=False), inc=False)
                T.op(lambda e, bank=bank, osl=osl: e.matmul(
                    ps[bank][:, :], onesml[32:33, 0:128],
                    rrow(32, osl * 512, (osl + 1) * 512),
                    start=False, stop=True))
                T.mark(f"t:cu:{i}")
                seq[0] += 1

        if SUB4 < 4:
            return
        def rot_mm(t):
            obuf = t % 2
            T.w(f"y:xnbo:{t}")
            T.w(f"v:gbf:{t}")
            if t == 0:
                T.w("v:vqctcp:1")
            elif t == 1:
                T.w("v:nwbcp:1")
            else:
                T.w(f"v:rotbf:{t - 2}")
            T.op(lambda e, t=t, obuf=obuf: e.matmul(
                ps[6 + t % 2][0:OM, :], gbf[:, t * OM:(t + 1) * OM],
                xnb[0:OM, obuf * 4096:obuf * 4096 + 512],
                start=True, stop=True))
            T.mark(f"t:rot:{t}")

        for t in range(B):
            obuf = t % 2
            if t == 0:
                rot_mm(0)
            T.w(f"v:rotbf:{t}")
            if SUB4 == 4 and SUB4R < 1:
                continue
            rslot = rotb if t % 2 == 0 else rotb2
            for tc in range(NTC):
                for osl in range(2):
                    i = seq[0]
                    bank = i % 6
                    bankwait(i)
                    T.w("g:oowa")
                    T.w(f"v:vqo:{t}")
                    _m = [("rot",), ("vqr",)]
                    for _j, _mm in enumerate(_m):
                        _st = (_j == 0)
                        _sp = (_j == len(_m) - 1)
                        if _mm[0] == "rot":
                            T.op(lambda e, bank=bank, tc=tc, osl=osl,
                                 rslot=rslot, _st=_st, _sp=_sp: e.matmul(
                                     ps[bank][:, :],
                                     rslot[:, tc * 128:(tc + 1) * 128],
                                     oow_a[:, osl * 512:(osl + 1) * 512],
                                     start=_st, stop=_sp), inc=_sp)
                        else:
                            T.op(lambda e, bank=bank, osl=osl, _st=_st,
                                 _sp=_sp: e.matmul(
                                     ps[bank][:, :], onesml[64:65, 0:128],
                                     vqr_a[64:65, osl * 512:(osl + 1) * 512],
                                     start=_st, stop=_sp), inc=_sp)
                    T.mark(f"t:sp:{t}:{tc}:{osl}")
                    seq[0] += 1
            if t + 1 < B:
                rot_mm(t + 1)
            if SUB4 < 5:
                continue
            for tc in range(NTC):
                for osl in range(2):
                    i = seq[0]
                    bank = i % 6
                    bankwait(i)
                    T.w("g:woa")
                    for kk in range(NK):
                        T.op(lambda e, bank=bank, kk=kk, tc=tc, osl=osl,
                             obuf=obuf: e.matmul(
                                 ps[bank][:, :],
                                 xnb[:, obuf * 4096 + kk * 512 + tc * 128:
                                     obuf * 4096 + kk * 512 + tc * 128 + 128],
                                 wo_a[:, kk * D + osl * 512:
                                      kk * D + (osl + 1) * 512],
                                 start=(kk == 0), stop=(kk == NK - 1)),
                             inc=(kk == NK - 1))
                    T.mark(f"t:h:{t}:{tc}:{osl}")
                    seq[0] += 1
        T.mark("t:done")

    def stream_scalar(eng=None):
        sch.stream("aE", eng)
        A_ = sch

        def own_b(b):
            if b == 0:
                for tc in range(NTC):
                    if tc >= 2:
                        A_.w(f"a:ssq:{b}:{tc - 2}")
                    dstb = xs1 if tc % 2 == 0 else xs1b
                    qn = "S1" if tc % 2 == 0 else "S1B"
                    A_.dma(qn, lambda e, b=b, tc=tc, dstb=dstb: e.dma_start(
                        out=dstb[:, :],
                        in_=io["XS"][b, tc * 128:(tc + 1) * 128, :]))
                    sch.mark(f"q:xs:{b}:{tc}", qn)
                    A_.w(f"v:xsum:{b}:{tc}")
                    A_.op(lambda e, tc=tc, dstb=dstb: e.activation(
                        dstb[:, :], dstb[:, :], AF.Square,
                        accum_out=stat[:, 384 + tc:384 + tc + 1]))
                    A_.mark(f"a:ssq:{b}:{tc}")
            def scw(tc):
                A_.w(f"v:xntok:{b}:{tc}")
                A_.dma(f"W{tc % 2}", lambda e, b=b, tc=tc: e.dma_start(
                    out=io["xn_sc"][b, tc * 128:(tc + 1) * 128, :],
                    in_=xtk[:, (tc % 2) * D:(tc % 2 + 1) * D]))
                sch.mark(f"y:scw:{b}:{tc}", f"W{tc % 2}")

            for tc in range(NTC):
                if tc >= 2:
                    A_.w(f"v:xntok:{b}:{tc - 2}")
                elif b > 0:
                    A_.w(f"v:xntok:{b - 1}:{tc + 2}")
                dst = xs1 if tc % 2 == 0 else xs1b
                qn = "S1" if tc % 2 == 0 else "S1B"
                A_.dma(qn, lambda e, b=b, tc=tc, dst=dst: e.dma_start(
                    out=dst[:, :],
                    in_=io["XS"][b, tc * 128:(tc + 1) * 128, :]))
                sch.mark(f"q:xsr:{b}:{tc}", qn)
                if tc >= 1:
                    scw(tc - 1)
            scw(NTC - 1)
            sch.mark(f"y:scwb0:{b}", "W0")
            sch.mark(f"y:scwb1:{b}", "W1")

        def do_unit_act(u):
            if u[0] == "p1t":
                ob, j = u[1], u[2]
                oc = (ob % 4) * 32
                A_.w(f"v:hsum:{ob}:{j}")
                b4 = tile_seq[(u[0], ob, j)] % 4
                A_.op(lambda e, b4=b4, oc=oc, j=j: e.activation(
                    xh2[:, b4 * D:(b4 + 1) * D], xh2[:, b4 * D:(b4 + 1) * D],
                    AF.Square,
                    accum_out=stat[:, 256 + oc + j:256 + oc + j + 1]))
                A_.mark(f"a:hssq:{ob}:{j}")
            elif u[0] == "o1t":
                b, i = u[1], u[2]
                col = 96 + 8 * (b % 2) + 4 + i
                A_.w(f"v:hsumo:{b}:{i}")
                b4 = tile_seq[(u[0], b, i)] % 4
                A_.op(lambda e, b4=b4, col=col: e.activation(
                    xh2[:, b4 * D:(b4 + 1) * D], xh2[:, b4 * D:(b4 + 1) * D],
                    AF.Square, accum_out=stat[:, col:col + 1]))
                A_.mark(f"a:ssqo:{b}:{i}")

        own_b(0)
        for t in range(B):
            for k in range(NK):
                for g in range(3):
                    A_.w(f"t:mm:{t}:{k}:{g}")
                    if g == 0:
                        if k > 0:
                            A_.w(f"v:cup:{t}:{k - 1}")
                        elif t > 0:
                            A_.w(f"v:cup:{t - 1}:{NK - 1}")
                    func = AF.Sigmoid if g < 2 else AF.Tanh
                    A_.op(lambda e, t=t, k=k, g=g, func=func: e.activation(
                        gst[:, g * TOK:(g + 1) * TOK],
                        ps[tidx(t, k, g) % 6][:, :], func,
                        bias=smal[:, 8 + g * 8 + k:8 + g * 8 + k + 1]))
                    A_.mark(f"a:ev:{t}:{k}:{g}")
                kind = slot_plan[(t, k)]
                if kind[0] == "own":
                    own_b(kind[1])
                else:
                    for u in kind[1]:
                        do_unit_act(u)
        for u in hm_tail:
            do_unit_act(u)

        if PHASE < 3:
            return
        A_.w("v:hsum:7:31")
        A_.w("y:cb")
        for j in range(2):
            A_.op(lambda e, j=j: e.activation(
                cbn[:, j * OM:(j + 1) * OM], cb2[:, j * OM:(j + 1) * OM],
                AF.Square, accum_out=smal[:, 200 + j:200 + j + 1]))
        A_.mark("a:cbsq")

        if PHASE < 4 or SUB4 < 3:
            return
        for t in range(B):
            if t == 0:
                A_.w("t:om3:7")
                A_.w("t:vqin")
                A_.w("t:vqct:1")
                A_.w("t:par:3")
                A_.w("v:gbf:7")
            if t > 0 and SUB4 >= 5:
                A_.w(f"p:out:{t - 1}:1")
            h0 = A_.dma("X0", lambda e, t=t: e.dma_start(
                out=xh2[:, 0:2 * D],
                in_=io["XS"][t, 0:256, :].rearrange("(c p) d -> p c d",
                                                    p=128)))
            if t > 0 and SUB4 >= 5:
                A_.w(f"p:out:{t - 1}:3")
            h1 = A_.dma("X1", lambda e, t=t: e.dma_start(
                out=xh2[:, 2 * D:4 * D],
                in_=io["XS"][t, 256:512, :].rearrange("(c p) d -> p c d",
                                                      p=128)))
            sch.labels[f"q:xth:{t}:0"] = ("X0", h0)
            sch.labels[f"q:xth:{t}:1"] = ("X1", h1)
            if SUB4 < 4 or (SUB4 == 4 and SUB4R < 2):
                continue
            for tc in range(NTC):
                A_.w(f"v:spadd:{t}:{tc}")
                A_.op(lambda e, tc=tc: e.activation(
                    xnh[:, 0:D], s_a[:, tc * D:(tc + 1) * D], AF.Square,
                    accum_out=stat[:, 496 + tc:496 + tc + 1]))
                A_.mark(f"a:ms:{t}:{tc}")
            if SUB4 < 5:
                continue
            for tc in range(NTC):
                for osl in range(2):
                    if osl == 0 and (t > 0 or tc > 0):
                        _pv = (t, tc - 1) if tc > 0 else (t - 1, NTC - 1)
                        A_.w(f"p:tm:{_pv[0]}:{_pv[1]}")
                    A_.w(f"v:hpre:{t}:{tc}:{osl}")
                    A_.op(lambda e, osl=osl: e.activation(
                        h_a[:, osl * 512:(osl + 1) * 512],
                        h_a[:, osl * 512:(osl + 1) * 512], AF.Tanh),
                        inc=False)
                    A_.op(lambda e, osl=osl: e.activation(
                        g_a[:, osl * 512:(osl + 1) * 512],
                        h_a[:, osl * 512:(osl + 1) * 512], AF.Sigmoid))
                    A_.mark(f"a:hg:{t}:{tc}:{osl}")
        A_.mark("a:done")

    def stream_vector(eng=None):
        sch.stream("vE", eng)
        V = sch

        def newton(dst_r, dst_negrm, src_sum, src_ssq, width, eps):
            mcol = (stat[:, NSC:NSC + width] if emit else None)
            v2 = (stat[:, NSC + 32:NSC + 32 + width] if emit else None)
            y = (stat[:, NSC + 64:NSC + 64 + width] if emit else None)
            V.op(lambda e: e.tensor_scalar(mcol, src_sum, 1.0 / D, None,
                                           ALU.mult), inc=False)
            V.op(lambda e: e.tensor_scalar(v2, src_ssq, 1.0 / D, None,
                                           ALU.mult), inc=False)
            V.op(lambda e: e.scalar_tensor_tensor(
                y, mcol, -1.0, mcol, ALU.mult, ALU.mult), inc=False)
            V.op(lambda e: e.tensor_add(v2, v2, y), inc=False)
            V.op(lambda e: e.tensor_scalar(v2, v2, eps, None, ALU.add),
                 inc=False)
            V.op(lambda e: e.tensor_scalar(y, v2, -0.5, 1.5, ALU.mult,
                                           ALU.add), inc=False)
            for _ in range(NEWT):
                V.op(lambda e: e.scalar_tensor_tensor(
                    dst_r, y, 1.0, y, ALU.mult, ALU.mult), inc=False)
                V.op(lambda e: e.scalar_tensor_tensor(
                    dst_r, dst_r, -0.5, v2, ALU.mult, ALU.mult), inc=False)
                V.op(lambda e: e.scalar_tensor_tensor(
                    y, dst_r, 1.5, y, ALU.add, ALU.mult), inc=False)
            V.op(lambda e: e.tensor_copy(dst_r, y), inc=False)
            V.op(lambda e: e.scalar_tensor_tensor(
                dst_negrm, y, -1.0, mcol, ALU.mult, ALU.mult))

        V.op(lambda e: e.memset(rows[0:1, 0:512], 1.0), inc=False)
        V.op(lambda e: e.memset(onesml[:, :], 1.0), inc=False)
        V.op(lambda e: e.memset(onesbf[0:1, :], 1.0))
        V.mark("v:ones")
        for k in range(NK):
            V.w(f"t:c0:{k}")
            V.op(lambda e, k=k: e.tensor_copy(
                cfp[:, k * 512:(k + 1) * 512], ps[6 + k % 2][:, :]),
                inc=False)
            V.op(lambda e, k=k: e.tensor_copy(
                cbf[:, k * 512:(k + 1) * 512], ps[6 + k % 2][:, :]))
            V.mark(f"v:c0cp:{k}")
        V.w("t:b3t")
        V.op(lambda e: e.tensor_copy(smal[:, 8:32], ps[7][0:128, 0:24]))
        V.mark("v:b3c")

        vunits = []

        def own_newt(b):
            V.w(f"a:ssq:{b}:{NTC - 1}")
            newton(stat[:, 32 + b * 4:32 + b * 4 + 4] if emit else None,
                   stat[:, b * 4:b * 4 + 4] if emit else None,
                   stat[:, 388:392] if emit else None,
                   stat[:, 384:388] if emit else None, 4, LN_EPS)
            V.mark(f"v:newt:own:{b}")

        def hm_newt(ob):
            oc = (ob % 4) * 32
            V.w(f"a:hssq:{ob}:31")
            newton(stat[:, 256 + oc:256 + oc + 32] if emit else None,
                   stat[:, 128 + oc:128 + oc + 32] if emit else None,
                   stat[:, 128 + oc:128 + oc + 32] if emit else None,
                   stat[:, 256 + oc:256 + oc + 32] if emit else None,
                   32, LN_EPS)
            if ob >= 2:
                V.w(f"t:hs:{ob - 2}:31")
            V.op(lambda e, ob=ob, oc=oc: e.tensor_copy(
                rbf[:, (ob % 2) * 32:(ob % 2) * 32 + 32],
                stat[:, 256 + oc:256 + oc + 32]), inc=False)
            if ob > 0:
                V.w(f"p:negs:{ob - 1}")
            V.op(lambda e, oc=oc: e.tensor_reduce(
                stat[:, 488:489], stat[:, 128 + oc:128 + oc + 32],
                axis=AX.X, op=ALU.add))
            V.mark(f"v:negsum:{ob}")
            V.mark(f"v:rbf:{ob}")

        def own_dve(b):
            if b == 0:
                for tc in range(NTC):
                    V.w(f"q:xs:{b}:{tc}")
                    srcb = xs1 if tc % 2 == 0 else xs1b
                    V.op(lambda e, tc=tc, srcb=srcb: e.tensor_reduce(
                        stat[:, 388 + tc:388 + tc + 1], srcb[:, :], axis=AX.X,
                        op=ALU.add))
                    V.mark(f"v:xsum:{b}:{tc}")
                own_newt(b)
            for tc in range(NTC):
                if tc >= 2:
                    V.w(f"y:scw:{b}:{tc - 2}")
                elif b > 0:
                    V.w(f"y:scw:{b - 1}:{tc + 2}")
                V.w(f"q:xsr:{b}:{tc}")
                srcb = xs1 if tc % 2 == 0 else xs1b
                V.op(lambda e, b=b, tc=tc, srcb=srcb: e.tensor_scalar(
                    xtk[:, (tc % 2) * D:(tc % 2 + 1) * D], srcb[:, :],
                    stat[:, 32 + b * 4 + tc:32 + b * 4 + tc + 1],
                    stat[:, b * 4 + tc:b * 4 + tc + 1],
                    ALU.mult, ALU.add))
                V.mark(f"v:xntok:{b}:{tc}")

        def hsevac(ob):
            V.w(f"t:hs:{ob}:31")
            V.w(f"p:negs:{ob}")
            if ob > 0:
                V.w(f"y:shd:{ob - 1}")
            V.op(lambda e: e.tensor_scalar(
                xnh[0:1, 0:512], ps[6][0:1, 0:512], stat[0:1, 489:490],
                None, ALU.add), inc=False)
            V.op(lambda e: e.tensor_scalar(
                xnh[0:1, 512:1024], ps[7][0:1, 0:512], stat[0:1, 489:490],
                None, ALU.add))
            V.mark(f"v:hsevac:{ob}")

        def do_unit_dve(u):
            if u[0] == "p1t":
                ob, j = u[1], u[2]
                oc = (ob % 4) * 32
                b4 = tile_seq[(u[0], ob, j)] % 4
                V.w(f"q:x1:{ob}:{j}")
                V.op(lambda e, b4=b4, oc=oc, j=j: e.tensor_reduce(
                    stat[:, 128 + oc + j:128 + oc + j + 1],
                    xh2[:, b4 * D:(b4 + 1) * D], axis=AX.X, op=ALU.add))
                V.mark(f"v:hsum:{u[1]}:{u[2]}")
            elif u[0] == "o1t":
                b, i = u[1], u[2]
                col = 96 + 8 * (b % 2) + i
                b4 = tile_seq[(u[0], b, i)] % 4
                V.w(f"q:xo:{b}:{i}")
                V.op(lambda e, b4=b4, col=col: e.tensor_reduce(
                    stat[:, col:col + 1], xh2[:, b4 * D:(b4 + 1) * D],
                    axis=AX.X, op=ALU.add))
                V.mark(f"v:hsumo:{b}:{i}")
            elif u[0] == "onewt":
                b = u[1]
                co = 96 + 8 * (b % 2)
                V.w(f"a:ssqo:{b}:3")
                newton(stat[:, 32 + b * 4:32 + b * 4 + 4] if emit else None,
                       stat[:, b * 4:b * 4 + 4] if emit else None,
                       stat[:, co:co + 4] if emit else None,
                       stat[:, co + 4:co + 8] if emit else None, 4, LN_EPS)
                V.mark(f"v:newt:own:{b}")
            elif u[0] == "newt":
                hm_newt(u[1])
            elif u[0] == "p2t" and u[2] == 31:
                hsevac(u[1])

        own_dve(0)

        for t in range(B):
            par = t % 2
            npar = 1 - par
            for k in range(NK):
                fb = tidx(t, k, 0) % 6
                V.w(f"a:ev:{t}:{k}:2")
                V.op(lambda e, fb=fb, k=k: e.tensor_mul(
                    ps[fb][:, :], gst[:, 0:TOK],
                    cfp[:, k * 512:(k + 1) * 512]), inc=False)
                V.op(lambda e, k=k: e.scalar_tensor_tensor(
                    cfp[:, k * 512:(k + 1) * 512],
                    gst[:, TOK:2 * TOK], 1.0,
                    gst[:, 2 * TOK:3 * TOK], ALU.mult, ALU.mult), inc=False)
                V.op(lambda e, k=k, fb=fb: e.tensor_add(
                    cfp[:, k * 512:(k + 1) * 512],
                    cfp[:, k * 512:(k + 1) * 512], ps[fb][:, :]), inc=False)
                V.op(lambda e, k=k, npar=npar: e.tensor_copy(
                    cbf[:, npar * 4096 + k * 512:npar * 4096 + (k + 1) * 512],
                    cfp[:, k * 512:(k + 1) * 512]))
                V.mark(f"v:cup:{t}:{k}")
                kind = slot_plan[(t, k)]
                if kind[0] == "own":
                    own_dve(kind[1])
                else:
                    for u in kind[1]:
                        do_unit_dve(u)
        for u in hm_tail:
            do_unit_dve(u)

        if PHASE < 2 or (PHASE == 2 and SUB2 < 2):
            return
        if PHASE >= 3:
            V.w("a:hssq:7:31")
            V.op(lambda e: e.memset(omu[:, :, :], 0.0))
            V.mark("v:omz")

        for bidx in range(2 * HB):
            V.w(f"t:hmT:{bidx}")
            for kk in range(NK):
                V.op(lambda e, bidx=bidx, kk=kk: e.tensor_scalar(
                    smal[:, 128 + kk * 8 + bidx:128 + kk * 8 + bidx + 1],
                    ps[1][0:128, kk:kk + 1], 1.0 / L, None, ALU.mult),
                    inc=(kk == NK - 1))
            V.mark(f"v:hmb:{bidx}")
        V.mark("v:hm")
        V.op(lambda e: e.tensor_copy(hmbf[:, :], smal[:, 128:192]))
        V.mark("v:hmbf")
        if PHASE < 3:
            return
        V.w("t:mixb")
        V.op(lambda e: e.tensor_copy(smal[:, 0:2], ps[7][:, 0:2]))
        V.mark("v:mixcp")
        V.op(lambda e: e.tensor_scalar(
            rows[64:65, 0:D], rows[64:65, 0:D], smal[64:65, 0:1], None,
            ALU.mult))
        V.mark("v:oobs")
        for q in range(4):
            V.w(f"t:par:{q}")
            V.op(lambda e, q=q: e.tensor_copy(
                par_sb[0:B, q * 504:(q + 1) * 504], ps[2 + q][0:B, 0:504]))
            V.mark(f"v:pst:{q}")
        for b in range(B):
            V.w(f"t:omT:{b}")
            V.op(lambda e, b=b: e.tensor_sub(
                omu[0:OM, b, :], omu[0:OM, b, :], ps[7][0:OM, 0:OM]),
                inc=False)
            V.op(lambda e, b=b: e.tensor_scalar(
                om2[0:OM, b * OM:(b + 1) * OM], omu[0:OM, b, :], -1.0, None,
                ALU.mult))
            V.mark(f"v:omfull:{b}")
            V.w(f"t:om2:{b}")
            V.op(lambda e, b=b: e.tensor_copy(
                om2[0:OM, b * OM:(b + 1) * OM], ps[6][0:OM, 0:OM]))
            V.mark(f"v:om2cp:{b}")
            V.w(f"t:om3:{b}")
            V.op(lambda e, b=b: e.scalar_tensor_tensor(
                gmat[0:OM, b * OM:(b + 1) * OM],
                om2[0:OM, b * OM:(b + 1) * OM], 0.5,
                omu[0:OM, b, :], ALU.mult, ALU.add), inc=False)
            V.op(lambda e, b=b: e.scalar_tensor_tensor(
                gmat[0:OM, b * OM:(b + 1) * OM], ps[7][0:OM, 0:OM],
                1.0 / 6.0, gmat[0:OM, b * OM:(b + 1) * OM],
                ALU.mult, ALU.add), inc=False)
            V.op(lambda e, b=b: e.tensor_add(
                gmat[0:OM, b * OM:(b + 1) * OM],
                gmat[0:OM, b * OM:(b + 1) * OM], idt[0:OM, 0:OM]),
                inc=False)
            V.op(lambda e, b=b: e.tensor_copy(
                gbf[:, b * OM:(b + 1) * OM], gmat[0:OM, b * OM:(b + 1) * OM]))
            V.mark(f"v:gbf:{b}")
        V.w("a:cbsq")
        sc = smal
        V.op(lambda e: e.tensor_scalar(sc[:, 202:204], sc[:, 200:202],
                                       1.0 / OM, None, ALU.mult), inc=False)
        V.op(lambda e: e.tensor_scalar(sc[:, 204:206], sc[:, 202:204],
                                       -0.5, 1.5, ALU.mult, ALU.add),
             inc=False)
        for _ in range(NEWT):
            V.op(lambda e: e.scalar_tensor_tensor(
                sc[:, 206:208], sc[:, 204:206], 1.0, sc[:, 204:206],
                ALU.mult, ALU.mult), inc=False)
            V.op(lambda e: e.scalar_tensor_tensor(
                sc[:, 206:208], sc[:, 206:208], -0.5, sc[:, 202:204],
                ALU.mult, ALU.mult), inc=False)
            V.op(lambda e: e.scalar_tensor_tensor(
                sc[:, 204:206], sc[:, 206:208], 1.5, sc[:, 204:206],
                ALU.add, ALU.mult), inc=False)
        V.op(lambda e: e.tensor_scalar(sc[:, 204:206], sc[:, 204:206],
                                       0.125, None, ALU.mult), inc=False)
        for j in range(2):
            V.op(lambda e, j=j: e.tensor_scalar(
                cbn[:, j * OM:(j + 1) * OM], cb2[:, j * OM:(j + 1) * OM],
                sc[:, 204 + j:205 + j], None, ALU.mult), inc=(j == 1))
        V.mark("v:cbn")
        V.w("t:vqin")
        V.op(lambda e: e.tensor_copy(smal[0:B, 32:96], ps[0][0:B, 0:OM]))
        V.mark("v:vqi")
        for j in range(2):
            V.w(f"t:cbnT:{j}")
            V.op(lambda e, j=j: e.tensor_copy(
                cbnT[:, j * 128:(j + 1) * 128], ps[1][0:OM, 0:128]))
            V.mark(f"v:cbnTcp:{j}")
        V.w("t:vqiT")
        V.op(lambda e: e.tensor_copy(smal[0:OM, 104:112], ps[2][0:OM, 0:B]))
        V.mark("v:vqiTcp")
        V.w("t:scores")
        V.op(lambda e: e.tensor_copy(stat[0:B, 128:384],
                                     ps[3][0:B, 0:NCODE]), inc=False)
        V.op(lambda e: e.reduce_max(smal[0:B, 2:3], stat[0:B, 128:384],
                                    axis=AX.X), inc=False)
        V.op(lambda e: e.tensor_scalar(stat[0:B, 128:384],
                                       stat[0:B, 128:384],
                                       smal[0:B, 2:3], None, ALU.is_ge))
        V.mark("v:oh")
        V.w("t:ohT")
        V.op(lambda e: e.tensor_copy(smal[:, 112:128], ps[4][0:128, 0:16]))
        V.mark("v:ohTcp")
        V.w("t:code")
        V.op(lambda e: e.tensor_copy(codbf[:, 0:8], ps[5][0:OM, 0:B]))
        V.mark("v:codecp")
        for osl in range(2):
            V.w(f"t:vqct:{osl}")
            V.op(lambda e, osl=osl: e.tensor_scalar(
                vqct[:, osl * 512:(osl + 1) * 512], ps[6][0:B, 0:512],
                smal[0:B, 1:2], None, ALU.mult))
            V.mark(f"v:vqctcp:{osl}")
        for osl in range(2):
            V.w(f"t:nwb:{osl}")
            V.op(lambda e, osl=osl: e.tensor_copy(
                nwb[:, osl * 512:(osl + 1) * 512], ps[7][:, :]))
            V.mark(f"v:nwbcp:{osl}")

        if PHASE < 4 or SUB4 < 1:
            return
        for i in range(2 * NTC):
            V.w(f"t:cu:{i}")
            tc, osl = i // 2, i % 2
            V.op(lambda e, i=i, tc=tc, osl=osl: e.tensor_copy(
                cu_a[:, tc * D + osl * 512:tc * D + (osl + 1) * 512],
                ps[i % 6][:, :]))
            V.mark(f"v:bfree:{i}")

        if SUB4 < 4:
            return
        seq = 2 * NTC

        def rotbf_op(t):
            V.w(f"t:rot:{t}")
            dstr = rotb if t % 2 == 0 else rotb2
            V.op(lambda e, t=t, dstr=dstr: e.tensor_scalar(
                dstr[:, :], ps[6 + t % 2][0:OM, :], smal[0:OM, 0:1], None,
                ALU.mult))
            V.mark(f"v:rotbf:{t}")
            V.w(f"y:vqr:{t}")
            V.op(lambda e: e.tensor_add(
                vqr_a[64:65, 0:D], vqr_a[64:65, 0:D], rrow(64, 0, D)))
            V.mark(f"v:vqo:{t}")

        for t in range(B):
            if t == 0:
                rotbf_op(0)
            if SUB4 == 4 and SUB4R < 1:
                continue
            for tc in range(NTC):
                if t > 0:
                    V.w(f"p:tm:{t - 1}:{tc}")
                V.w(f"q:xth:{t}:{1 if tc >= 2 else 0}")
                V.op(lambda e, t=t, tc=tc: e.tensor_scalar(
                    xt_a[:, :], xh2[:, tc * D:(tc + 1) * D],
                    stat[:, 32 + t * 4 + tc:32 + t * 4 + tc + 1],
                    stat[:, t * 4 + tc:t * 4 + tc + 1],
                    ALU.mult, ALU.add), inc=False)
                for osl in range(2):
                    i = seq
                    V.w(f"t:sp:{t}:{tc}:{osl}")
                    V.op(lambda e, i=i, tc=tc, osl=osl: e.tensor_add(
                        s_a[:, tc * D + osl * 512:tc * D + (osl + 1) * 512],
                        ps[i % 6][:, :], xt_a[:, osl * 512:(osl + 1) * 512]))
                    V.mark(f"v:bfree:{i}")
                    if osl == 1:
                        V.mark(f"v:spadd:{t}:{tc}")
                    seq += 1
            if SUB4 == 4 and SUB4R < 2:
                continue
            V.w(f"a:ms:{t}:{NTC - 1}")
            msc = stat[:, 496:500] if emit else None
            yy = (stat[:, NSC + 64:NSC + 68] if emit else None)
            r2 = smal[:, 192:196] if emit else None
            V.op(lambda e: e.tensor_scalar(msc, msc, 1.0 / D, RMS_EPS,
                                           ALU.mult, ALU.add), inc=False)
            V.op(lambda e: e.tensor_scalar(yy, msc, -0.5, 1.5, ALU.mult,
                                           ALU.add), inc=False)
            for _ in range(NEWT):
                V.op(lambda e: e.scalar_tensor_tensor(
                    r2, yy, 1.0, yy, ALU.mult, ALU.mult), inc=False)
                V.op(lambda e: e.scalar_tensor_tensor(
                    r2, r2, -0.5, msc, ALU.mult, ALU.mult), inc=False)
                V.op(lambda e: e.scalar_tensor_tensor(
                    yy, r2, 1.5, yy, ALU.add, ALU.mult), inc=False)
            V.op(lambda e: e.tensor_copy(r2, yy))
            V.mark(f"v:rstd:{t}")
            if t + 1 < B:
                rotbf_op(t + 1)
            if SUB4 < 5:
                continue
            for tc in range(NTC):
                for osl in range(2):
                    i = seq
                    V.w(f"t:h:{t}:{tc}:{osl}")
                    V.op(lambda e, i=i, tc=tc, osl=osl: e.tensor_add(
                        h_a[:, osl * 512:(osl + 1) * 512], ps[i % 6][:, :],
                        cu_a[:, tc * D + osl * 512:tc * D + (osl + 1) * 512]))
                    V.mark(f"v:bfree:{i}")
                    V.mark(f"v:hpre:{t}:{tc}:{osl}")
                    seq += 1
                V.w(f"a:hg:{t}:{tc}:1")
                V.op(lambda e: e.tensor_mul(g_a[:, :], g_a[:, :], h_a[:, :]),
                     inc=False)
                V.op(lambda e, tc=tc: e.scalar_tensor_tensor(
                    g_a[:, :], g_a[:, :], smal[:, 192 + tc:192 + tc + 1],
                    nwb[:, :], ALU.mult, ALU.mult))
                V.mark(f"v:gsc:{t}:{tc}")
        V.mark("v:done")

    def stream_pool(eng=None):
        sch.stream("pE", eng)
        P = sch

        for kk in range(NK):
            P.dma("GW", lambda e, kk=kk: e.dma_start(
                out=w3[:, kk * 3072:(kk + 1) * 3072],
                in_=io["W3T"][kk * 128:(kk + 1) * 128, :]))
        for kk in range(NK):
            P.dma("GU", lambda e, kk=kk: e.dma_start(
                out=u3[:, kk * 3072:(kk + 1) * 3072],
                in_=io["U3T"][kk * 128:(kk + 1) * 128, :]))
        for kk in range(NK):
            sch.mark(f"g:w3:{kk}", "GW")
            sch.mark(f"g:u3:{kk}", "GU")

        for ob in range(B):
            for j in range(32):
                if j == 0:
                    P.w(f"v:negsum:{ob}")
                    if ob > 0:
                        P.w(f"v:hsevac:{ob - 1}")
                    P.op(lambda e: e.tensor_reduce(
                        stat[0:1, 489:490], stat[:, 488:489], axis=AX.C,
                        op=ALU.add))
                    P.mark(f"p:negs:{ob}")
                jg = ob * 32 + j
                if jg >= 3:
                    pj = jg - 3
                    P.w(f"t:hs:{pj // 32}:{pj % 32}")
                dst = (xhb[:, (jg % 3) * 1024:(jg % 3 + 1) * 1024]
                       if jg % 3 < 2 else bstg[:, 0:1024]) if emit else None
                P.dma(f"XB{jg % 2}", lambda e, ob=ob, j=j, dst=dst:
                      e.dma_start(
                          out=dst,
                          in_=io["XH"][ob, j * 128:(j + 1) * 128, :]))
                sch.mark(f"g:xb:{ob}:{j}", f"XB{jg % 2}")

        if PHASE < 3:
            return
        P.w("a:hssq:7:31")
        P.w("v:hsum:7:31")
        v = P.dma("GP", lambda e: e.dma_start(out=opbr[0:1, :],
                                              in_=io["OPB"][:, :]))
        P.wv("GP", v)
        sch.mark("y:opb", "GP")
        v = P.dma("GP", lambda e: e.dma_start(out=pwt[:, :],
                                              in_=io["PWT"][:, :]))
        P.wv("GP", v)
        sch.mark("y:pwt", "GP")
        P.w("v:cup:7:7")
        for kk in range(NK):
            if kk > 0:
                P.w(f"t:parchunk:{kk - 1}")
            P.dma("GP", lambda e, kk=kk: e.dma_start(
                out=opwb[:, :], in_=io["OPWT"][kk * 128:(kk + 1) * 128, :]))
            sch.mark(f"g:opw:{kk}", "GP")
        if PHASE < 4:
            return
        P.w("t:step:7")
        for c in range(NK):
            v = P.dma("GP", lambda e, c=c: e.dma_start(
                out=uo_a[:, c * D:(c + 1) * D],
                in_=io["UOTR"][:, c * D:(c + 1) * D]))
        P.wv("GP", v)
        sch.mark("g:uoa", "GP")
        for c in range(NK):
            v = P.dma("GP", lambda e, c=c: e.dma_start(
                out=wo_a[:, c * D:(c + 1) * D],
                in_=io["WOTR"][:, c * D:(c + 1) * D]))
        P.wv("GP", v)
        sch.mark("g:woa", "GP")
        v = P.dma("GP", lambda e: e.dma_start(out=oow_a[:, :],
                                              in_=io["OOWT"][:, :]))
        P.wv("GP", v)
        sch.mark("g:oowa", "GP")
        for t in range(B):
            for tc in range(NTC):
                P.w(f"v:gsc:{t}:{tc}")
                if tc >= 2:
                    P.w(f"y:st:{t}:{tc - 2}")
                elif t > 0:
                    P.w(f"y:st:{t - 1}:{tc + 2}")
                P.w(f"q:xth:{t}:{1 if tc >= 2 else 0}")
                P.op(lambda e, tc=tc: e.tensor_mul(
                    tmp_a[:, (tc % 2) * D:(tc % 2 + 1) * D], g_a[:, :],
                    s_a[:, tc * D:(tc + 1) * D]))
                P.mark(f"p:tm:{t}:{tc}")
                P.op(lambda e, tc=tc: e.tensor_add(
                    tmp_a[:, (tc % 2) * D:(tc % 2 + 1) * D],
                    tmp_a[:, (tc % 2) * D:(tc % 2 + 1) * D],
                    xh2[:, tc * D:(tc + 1) * D]))
                P.mark(f"p:out:{t}:{tc}")
        P.mark("p:done", "pE")

    def stream_sync(eng=None):
        sch.stream("qYop", eng)
        Y = sch

        def d(fn):
            return Y.dma("qY", fn)

        for fn in [
            lambda e: e.dma_start(out=rows[32:33, 0:D], in_=io["BO"][:, :]),
            lambda e: e.dma_start(out=rows[32:33, D:2 * D], in_=io["NW"][:, :]),
            lambda e: e.dma_start(out=rows[64:65, 0:D], in_=io["OOB"][:, :]),
            lambda e: e.dma_start(out=rows[64:65, D:2 * D], in_=io["PB"][:, :]),
            lambda e: e.dma_start(out=rows[0:1, 512:512 + OM],
                                  in_=io["VQB"][:, :]),
            lambda e: e.dma_start(out=rows[0:1, 1024:2048],
                                  in_=io["INC"][:, :]),
            lambda e: e.dma_start(out=smal[0:1, 0:2], in_=io["MIX"][:, :]),
            lambda e: e.dma_start(out=idt[:, :], in_=io["IDT"][:, :]),
            lambda e: e.dma_start(
                out=cb2[:, :],
                in_=io["CB"].rearrange("(c p) e -> p c e", p=128)),
            lambda e: e.dma_start(out=xh2[0:1, 0:3 * D], in_=io["B3"][:, :]),
        ]:
            Y.dma("Y0", fn)
        for lbl in ["y:rows", "y:mix", "y:idt", "y:cb", "y:b3"]:
            Y.mark(lbl, "Y0")

        def xnbt(t):
            Y.w(f"y:scwb0:{t}")
            Y.w(f"y:scwb1:{t}")
            if t >= 2:
                Y.w(f"t:step:{t - 2}")
            for kk in range(NK):
                Y.dma(f"N{t % 2}", lambda e, t=t, kk=kk: e.dma_start_transpose(
                    xnb[:, (t % 2) * 4096 + kk * 512:
                        (t % 2) * 4096 + (kk + 1) * 512],
                    io["xn_sc"][t, :, kk * 128:(kk + 1) * 128]))
            Y.mark(f"y:xnb:{t}", f"N{t % 2}")

        def shd(ob):
            Y.w(f"v:hsevac:{ob}")
            Y.dma("H0", lambda e, ob=ob: e.dma_start(
                out=io["hm_sc"][ob:ob + 1, :], in_=xnh[0:1, 0:D]))
            Y.mark(f"y:shd:{ob}", "H0")

        def p1load(u):
            kindu, ob, j = u
            ts = tile_seq[(kindu, ob, j)]
            if ts < 3:
                Y.w("t:b3t")
            if ts >= 4:
                Y.w(tile_done_label(tile_list[ts - 4]))
            srcio = (io["XH"] if kindu == "p1t" else io["XS"]) if emit \
                else None
            Y.dma(f"XL{ts % 4}", lambda e, ob=ob, j=j, ts=ts, srcio=srcio:
                  e.dma_start(
                      out=xh2[:, (ts % 4) * D:(ts % 4 + 1) * D],
                      in_=srcio[ob, j * 128:(j + 1) * 128, :]))
            lbl = f"q:x1:{ob}:{j}" if kindu == "p1t" else f"q:xo:{ob}:{j}"
            sch.mark(lbl, f"XL{ts % 4}")

        _order = [(_t, _k) for _t in range(B) for _k in range(NK)]
        due = {}
        for _i, _tk in enumerate(_order):
            kind = slot_plan[_tk]
            if kind[0] == "own" and kind[1] >= 1:
                _b = kind[1]
                _j = max(_i + 1, _order.index((_b - 1, 0)))
                _j = min(_j, len(_order) - 1)
                due.setdefault(_order[_j], []).append(("xnbt", _b))
            elif kind[0] == "units":
                for _u in kind[1]:
                    if _u[0] == "p2t" and _u[2] == 31:
                        _j = min(_i + 1, len(_order) - 1)
                        due.setdefault(_order[_j], []).append(("shd", _u[1]))
        placed = set()
        xnbt(0)
        for t in range(B):
            for k in range(NK):
                kind = slot_plan[(t, k)]
                if kind[0] == "units":
                    for u in kind[1]:
                        if u[0] in ("p1t", "o1t"):
                            p1load(u)
                for what, idx in due.get((t, k), []):
                    if what == "xnbt":
                        xnbt(idx)
                    else:
                        shd(idx)
                    placed.add((what, idx))
        for u in hm_tail:
            if u[0] in ("p1t", "o1t"):
                p1load(u)
        for ob in range(B):
            if ("shd", ob) not in placed:
                shd(ob)

        if PHASE < 2 or (PHASE == 2 and SUB2 < 1):
            return
        for bidx in range(2 * HB):
            if bidx == 0:
                Y.w(f"y:shd:{B - 1}")
            elif not (PHASE == 2 and SUB2 < 2):
                Y.w(f"t:hmT:{bidx - 1}")
            Y.dma("RB", lambda e, bidx=bidx: e.dma_start(
                out=xnh[0:1, 0:D], in_=io["hm_sc"][bidx:bidx + 1, :]))
            Y.mark(f"y:hmrow:{bidx}", "RB")

        if PHASE < 3:
            return
        Y.w("a:hssq:7:31")
        Y.w("v:hsum:7:31")
        Y.dma("VW", lambda e: e.dma_start(
            out=vqw[:, :],
            in_=io["VQWT"].rearrange("(k p) e -> p k e", p=128)))
        Y.mark("y:vqw", "VW")
        Y.w("v:pst:3")
        Y.w("v:omz")
        for r in range(63):
            ln = 63 - r
            off = _triu_off(r)
            Y.dma(f"XL{r % 4}", lambda e, r=r, ln=ln, off=off: e.dma_start(
                out=omu[r:r + 1, :, r + 1:64],
                in_=par_sb[0:B, off:off + ln],
                allow_slow_non_contiguous=(ln <= 2)))
        for _q in ["XL0", "XL1", "XL2", "XL3"]:
            sch.mark(f"g:scat:{_q}", _q)


        if PHASE < 4 or SUB4 < 2:
            return
        Y.w("v:vqctcp:1")
        Y.dma("VQ", lambda e: e.dma_start(out=io["vq_sc"][:, :],
                                          in_=vqct[:, :]))
        Y.mark("y:vqwr", "VQ")
        for t in range(B):
            Y.w("t:step:7")
            if t >= 2:
                if SUB4 >= 5:
                    Y.w(f"t:h:{t - 2}:{NTC - 1}:1")
                elif SUB4 == 4 and SUB4R >= 1:
                    Y.w(f"t:sp:{t - 2}:{NTC - 1}:1")
            for kk in range(NK):
                Y.dma(f"N{t % 2}", lambda e, t=t, kk=kk: e.dma_start_transpose(
                    xnb[:, (t % 2) * 4096 + kk * 512:
                        (t % 2) * 4096 + (kk + 1) * 512],
                    io["xn_sc"][t, :, kk * 128:(kk + 1) * 128]))
            Y.mark(f"y:xnbo:{t}", f"N{t % 2}")
            if t == 0:
                Y.w("y:vqwr")
            elif SUB4 >= 5 or (SUB4 == 4 and SUB4R >= 1):
                Y.w(f"t:sp:{t - 1}:{NTC - 1}:1")
            Y.dma("VQ", lambda e, t=t: e.dma_start(
                out=vqr_a[64:65, :], in_=io["vq_sc"][t:t + 1, :]))
            Y.mark(f"y:vqr:{t}", "VQ")
            if t > 0 and SUB4 >= 5:
                for tc in range(NTC):
                    Y.w(f"p:out:{t - 1}:{tc}")
                    Y.dma(f"ST{tc % 2}", lambda e, t=t, tc=tc: e.dma_start(
                        out=io["OUT"][t - 1, tc * 128:(tc + 1) * 128, :],
                        in_=tmp_a[:, (tc % 2) * D:(tc % 2 + 1) * D]))
                    Y.mark(f"y:st:{t - 1}:{tc}", f"ST{tc % 2}")
        if SUB4 < 5:
            return
        for tc in range(NTC):
            Y.w(f"p:out:{B - 1}:{tc}")
            Y.dma(f"ST{tc % 2}", lambda e, tc=tc: e.dma_start(
                out=io["OUT"][B - 1, tc * 128:(tc + 1) * 128, :],
                in_=tmp_a[:, (tc % 2) * D:(tc % 2 + 1) * D]))
            Y.mark(f"y:st:{B - 1}:{tc}", f"ST{tc % 2}")
        if DEBUG:
            Y.w("v:done")
            Y.w("a:done")
            for fn in [
                lambda e: e.dma_start(out=io["DBG_HM"][:, :],
                                      in_=smal[:, 128:192]),
                lambda e: e.dma_start(out=io["DBG_G"][:, :], in_=gmat[:, :]),
                lambda e: e.dma_start(out=io["DBG_VQ"][:, :], in_=vqct[:, :]),
                lambda e: e.dma_start(out=io["DBG_CF"][:, :],
                                      in_=cfp[:, 0:NK * TOK]),
                lambda e: e.dma_start(out=io["DBG_XNB"][:, :],
                                      in_=xnb[:, 0:NK * TOK]),
                lambda e: e.dma_start(out=io["DBG_S"][:, :], in_=s_a[:, :]),
            ]:
                v = Y.dma("RB", fn)
                Y.wv("RB", v)
        if emit:
            Y.cur.wait_ge(sch.sems["ST0"], sch.cnt.get("ST0", 0))
            Y.cur.wait_ge(sch.sems["ST1"], sch.cnt.get("ST1", 0))

    if emit:
        @block.tensor
        def _(eng):
            stream_tensor(eng)

        @block.scalar
        def _(eng):
            stream_scalar(eng)

        @block.vector
        def _(eng):
            stream_vector(eng)

        @block.gpsimd
        def _(eng):
            stream_pool(eng)

        @block.sync
        def _(eng):
            stream_sync(eng)
    else:
        stream_tensor()
        stream_scalar()
        stream_vector()
        stream_pool()
        stream_sync()
    ctx.close()


_NC_CACHE = [None]


def kernel(**inputs):
    x = np.ascontiguousarray(np.asarray(inputs["L0_attn_x"], np.float32))

    def T(a):
        return np.ascontiguousarray(np.asarray(a, np.float32).T)

    def R(a, shape):
        return np.ascontiguousarray(np.asarray(a, np.float32).reshape(shape))

    def KR(a):
        return np.ascontiguousarray(
            np.asarray(a, np.float32).T.reshape(NK, 128, D)
            .transpose(1, 0, 2).reshape(128, NK * D))

    w3t = np.ascontiguousarray(np.concatenate(
        [T(inputs["L0_ffn_wf"]), T(inputs["L0_ffn_wi"]),
         T(inputs["L0_ffn_wc"])], axis=1))
    u3t = np.ascontiguousarray(np.concatenate(
        [T(inputs["L0_ffn_uf"]), T(inputs["L0_ffn_ui"]),
         T(inputs["L0_ffn_uc"])], axis=1))
    b3 = np.ascontiguousarray(np.concatenate(
        [R(inputs["L0_ffn_bf"], (1, D)), R(inputs["L0_ffn_bi"], (1, D)),
         R(inputs["L0_ffn_bc"], (1, D))], axis=1))
    mix = np.array([[np.float32(inputs["L0_attn_omega_mix"]),
                     np.float32(inputs["L0_attn_vq_mix"])]], np.float32)
    common = {
        "W3T": w3t, "U3T": u3t,
        "WOTR": KR(inputs["L0_ffn_wo"]), "UOTR": KR(inputs["L0_ffn_uo"]),
        "B3": b3, "BO": R(inputs["L0_ffn_bo"], (1, D)),
        "INC": R(inputs["L0_ffn_in_c"], (1, D)),
        "OPWT": T(inputs["L0_attn_omega_proj_w"]),
        "OPB": R(inputs["L0_attn_omega_proj_b"], (1, NPAR)),
        "OOWT": T(inputs["L0_attn_omega_out_w"]),
        "OOB": R(inputs["L0_attn_omega_out_b"], (1, D)),
        "VQWT": T(inputs["L0_attn_vq_proj_in_w"]),
        "VQB": R(inputs["L0_attn_vq_proj_in_b"], (1, OM)),
        "PWT": T(inputs["L0_attn_vq_proj_out_w"]),
        "PB": R(inputs["L0_attn_vq_proj_out_b"], (1, D)),
        "CB": np.ascontiguousarray(
            np.asarray(inputs["L0_attn_codebook"], np.float32)),
        "NW": R(inputs["L0_attn_norm_w"], (1, D)),
        "MIX": mix, "XH": x,
        "IDT": np.eye(128, dtype=np.float32),
    }
    in_maps = []
    for c in range(NCORE):
        m = dict(common)
        m["XS"] = np.ascontiguousarray(x[:, c * TOK:(c + 1) * TOK, :])
        in_maps.append(m)

    if _NC_CACHE[0] is None:
        _NC_CACHE[0] = build_program()
    nc = _NC_CACHE[0]
    res = run_bass_kernel_spmd(nc, in_maps, core_ids=list(range(NCORE)))
    out = np.concatenate(
        [np.asarray(res.results[c]["OUT"]) for c in range(NCORE)], axis=1)
    return out


if __name__ == "__main__":
    print("building program...")
    prog = build_program()
    try:
        print("built; instructions:", prog.n_instructions())
    except Exception:
        print("built")
